# version 09
# baseline (speedup 1.0000x reference)
import numpy as np
import concourse.bass as bass
import concourse.mybir as mybir
from concourse.bass_utils import run_bass_kernel_spmd

F32 = mybir.dt.float32
BF16 = mybir.dt.bfloat16
AF = mybir.ActivationFunctionType
OP = mybir.AluOpType
AX = mybir.AxisListType.X

N = 4096
D = 1024
H = 8
I = 2816
NT = N // 128
DC = D // 128
G = 8
GW = 512
IC = I // 128

MIDS = [0.25, 0.75, 1.25, 1.75, 2.5, 3.5, 5.0]

ENGS = ("sync", "tensor", "vector", "scalar", "gpsimd")


class Sched:

    def __init__(self):
        self.ops = {e: [] for e in ENGS}
        self.cnt = {s: 0 for s in "dtvae"}
        self.pending_fence = {}

    def op(self, eng, fn, waits=(), inc=None, no_ldw=False):
        waits = [w for w in waits if w is not None]
        if eng in self.pending_fence:
            waits = list(self.pending_fence.pop(eng)) + waits
        mark = None
        if inc is not None:
            self.cnt[inc] += 16 if inc in "de" else 1
            mark = (inc, self.cnt[inc])
        self.ops[eng].append((fn, tuple(waits), inc, mark, no_ldw))
        return mark

    def fence(self):
        snap = tuple((s, v) for s, v in self.cnt.items() if v > 0)
        for e in ENGS:
            self.pending_fence[e] = snap

    def emit(self, eng_name, eng, handles):
        guaranteed = {}
        self.no_ldw_names = getattr(self, "no_ldw_names", set())
        for fn, waits, inc, mark, no_ldw in self.ops[eng_name]:
            for (sem, val) in waits:
                if guaranteed.get(sem, 0) < val:
                    eng.wait_ge(handles[sem], val)
                    guaranteed[sem] = val
            inst = fn(eng)
            if inc is not None:
                inst.then_inc(handles[inc], 16 if inc in "de" else 1)
            if no_ldw:
                self.no_ldw_names.add(inst.ins.name)


def build_program(debug=False):
    nc = bass.Bass("TRN2", target_bir_lowering=False)
    IK = "ExternalOutput" if debug else "Internal"

    x_in = nc.dram_tensor("x", [N, D], F32, kind="ExternalInput").ap()
    wq_in = nc.dram_tensor("wq", [512, D], F32, kind="ExternalInput").ap()
    wk_in = nc.dram_tensor("wk", [512, D], F32, kind="ExternalInput").ap()
    wv_in = nc.dram_tensor("wv", [H * D, D], F32, kind="ExternalInput").ap()
    wg_in = nc.dram_tensor("wg", [I, D], F32, kind="ExternalInput").ap()
    wu_in = nc.dram_tensor("wu", [I, D], F32, kind="ExternalInput").ap()
    wd_in = nc.dram_tensor("wd", [D, I], F32, kind="ExternalInput").ap()
    out_t = nc.dram_tensor("out", [N, D], F32, kind="ExternalOutput").ap()

    xn_scr = nc.dram_tensor("xn_scr", [N, D], BF16, kind=IK).ap()
    xq_scr = nc.dram_tensor("xq_scr", [N, D], BF16, kind=IK).ap()
    qt_scr = nc.dram_tensor("qt_scr", [512, N], BF16, kind=IK).ap()
    kt_scr = nc.dram_tensor("kt_scr", [512, N], BF16, kind=IK).ap()
    wq_bf = nc.dram_tensor("wq_bf", [512, D], BF16, kind=IK).ap()
    wk_bf = nc.dram_tensor("wk_bf", [512, D], BF16, kind=IK).ap()
    wv_bf = nc.dram_tensor("wv_bf", [H * D, D], BF16, kind=IK).ap()
    v_scr = nc.dram_tensor("v_scr", [H, 2, 128, NT, 512], BF16, kind=IK).ap()
    oacc_scr = nc.dram_tensor("oacc_scr", [N, D], F32, kind=IK).ap()
    wgq_scr = nc.dram_tensor("wgq_scr", [I, D], BF16, kind=IK).ap()
    wuq_scr = nc.dram_tensor("wuq_scr", [I, D], BF16, kind=IK).ap()
    wdq_scr = nc.dram_tensor("wdq_scr", [D, I], BF16, kind=IK).ap()
    hq_scr = nc.dram_tensor("hq_scr", [N, I], BF16, kind=IK).ap()

    S = Sched()
    from contextlib import ExitStack
    with ExitStack() as st:
        e = st.enter_context
        ones = e(nc.sbuf_tensor("ones", [128, 128], BF16))
        cneg = e(nc.sbuf_tensor("cneg", [128, 8], F32))
        qp = [e(nc.sbuf_tensor(f"qp{i}", [128, 1024], F32)) for i in range(5)]
        thrp = e(nc.sbuf_tensor("thrp", [128, 176], F32))
        thr6 = e(nc.sbuf_tensor("thr6", [128, 176], F32))
        tks = [e(nc.sbuf_tensor(f"tk{i}", [128, 176], F32)) for i in range(7)]
        smm = e(nc.sbuf_tensor("smm", [128, 64], F32))
        cbf = e(nc.sbuf_tensor("cbf", [128, 16], BF16))

        pS = [e(nc.psum_tensor(f"pS{i}", [128, 512], F32)) for i in range(4)]
        pO = [e(nc.psum_tensor(f"pO{i}", [128, 512], F32)) for i in range(3)]
        pDen = e(nc.psum_tensor("pDen", [128, 16], F32))

        def emit_quant(in_ap, out_ap, W, w_in=(), w_out=()):
            B = W // 16
            v3 = lambda t: t[:, 0:W].rearrange("p (b i) -> p b i", i=16)
            b3 = lambda t: t[:, 0:B].rearrange("p (b o) -> p b o", o=1)
            bb = lambda t: b3(t).broadcast_to((128, B, 16))
            in3 = in_ap.rearrange("p (b i) -> p b i", i=16)
            m_thr = S.op("vector", lambda v: v.tensor_reduce(
                b3(thrp), in3, axis=AX, op=OP.max, apply_absolute_value=True),
                waits=tuple(w_in), inc="v")
            m_t6 = S.op("vector", lambda v: v.tensor_single_scalar(
                thr6[:, 0:B], thrp[:, 0:B], float(np.float32(6e-12)), OP.max),
                waits=(m_thr,), inc="v")
            m_tk0 = None
            for k in range(7):
                m_tk0 = S.op("vector", lambda v, k=k: v.tensor_single_scalar(
                    tks[k][:, 0:B], thr6[:, 0:B], MIDS[k], OP.mult),
                    waits=(m_t6,), inc="v")
            m_abs = S.op("scalar", lambda a: a.activation(
                qp[0][:, 0:W], in_ap, AF.Abs),
                waits=tuple(w_in) + (m_thr,), inc="a")
            m_sgn = S.op("scalar", lambda a: a.sign(
                qp[1][:, 0:W], in_ap), inc="a")
            S.op("vector", lambda v: v.tensor_single_scalar(
                qp[0][:, 0:W], qp[0][:, 0:W], 6.0, OP.mult),
                waits=(m_abs,), inc="v")

            def mask(k, dst):
                S.op("vector", lambda v, k=k, dst=dst: v.tensor_tensor(
                    v3(dst), v3(qp[0]), bb(tks[k]), op=OP.is_gt),
                    waits=(m_tk0,), inc="v")
            mask(0, qp[2]); mask(1, qp[3])
            S.op("vector", lambda v: v.tensor_add(
                qp[2][:, 0:W], qp[2][:, 0:W], qp[3][:, 0:W]), inc="v")
            mask(2, qp[3]); mask(3, qp[4])
            S.op("vector", lambda v: v.tensor_add(
                qp[3][:, 0:W], qp[3][:, 0:W], qp[4][:, 0:W]), inc="v")
            S.op("vector", lambda v: v.tensor_add(
                qp[2][:, 0:W], qp[2][:, 0:W], qp[3][:, 0:W]), inc="v")
            mask(4, qp[3]); mask(5, qp[4])
            S.op("vector", lambda v: v.tensor_add(
                qp[3][:, 0:W], qp[3][:, 0:W], qp[4][:, 0:W]), inc="v")
            S.op("vector", lambda v: v.scalar_tensor_tensor(
                qp[2][:, 0:W], qp[2][:, 0:W], 0.5, qp[3][:, 0:W],
                op0=OP.mult, op1=OP.add), inc="v")
            mask(6, qp[3])
            S.op("vector", lambda v: v.scalar_tensor_tensor(
                qp[2][:, 0:W], qp[3][:, 0:W], 2.0, qp[2][:, 0:W],
                op0=OP.mult, op1=OP.add), inc="v")
            S.op("vector", lambda v: v.scalar_tensor_tensor(
                v3(qp[1]), v3(qp[1]), float(np.float32(1.0 / 6.0)), bb(thr6),
                op0=OP.mult, op1=OP.mult), waits=(m_sgn,), inc="v")
            m_out = S.op("vector", lambda v: v.tensor_mul(
                out_ap, qp[2][:, 0:W], qp[1][:, 0:W]),
                waits=tuple(w_out), inc="v")
            return {"out": m_out, "sgn": m_sgn}

        m_ones = S.op("vector", lambda v: v.memset(ones[:, :], 1.0), inc="v")

        stA = ExitStack()
        e = stA.enter_context
        a_xf = e(nc.sbuf_tensor("a_xf", [128, 2 * D], F32))
        a_xsq = e(nc.sbuf_tensor("a_xsq", [128, D], F32))
        a_st = e(nc.sbuf_tensor("a_st", [128, 80], F32))
        a_xnbf = e(nc.sbuf_tensor("a_xnbf", [128, 2 * D], BF16))
        a_xnf = e(nc.sbuf_tensor("a_xnf", [128, D], F32))
        a_xqbf = e(nc.sbuf_tensor("a_xqbf", [128, 2 * D], BF16))
        a_xnt = e(nc.sbuf_tensor("a_xnt", [128, DC * N], BF16))
        a_wvt = e(nc.sbuf_tensor("a_wvt", [128, 2 * DC * D], BF16))
        a_vout = e(nc.sbuf_tensor("a_vout", [128, 2 * 8 * 512], BF16))
        a_wqt = e(nc.sbuf_tensor("a_wqt", [128, DC * 512], BF16))
        a_wkt = e(nc.sbuf_tensor("a_wkt", [128, DC * 512], BF16))
        a_qke = e(nc.sbuf_tensor("a_qke", [128, 2 * 512], BF16))
        a_wcf = e(nc.sbuf_tensor("a_wcf", [128, 4 * D], F32))
        a_wcb = e(nc.sbuf_tensor("a_wcb", [128, 4 * D], BF16))

        cast_jobs = [(wq_in[s * 128:(s + 1) * 128, :],
                      wq_bf[s * 128:(s + 1) * 128, :]) for s in range(4)]
        cast_jobs += [(wk_in[s * 128:(s + 1) * 128, :],
                       wk_bf[s * 128:(s + 1) * 128, :]) for s in range(4)]
        cast_jobs += [(wv_in[s * 128:(s + 1) * 128, :],
                       wv_bf[s * 128:(s + 1) * 128, :]) for s in range(64)]
        cast_prev = [None] * 4
        cast_marks = []
        cast_i = [0]

        def emit_cast_one():
            j = cast_i[0]
            if j >= len(cast_jobs):
                return
            cast_i[0] = j + 1
            csrc, cdst = cast_jobs[j]
            cb = j % 4
            fs = a_wcf[:, cb * D:(cb + 1) * D]
            bs = a_wcb[:, cb * D:(cb + 1) * D]
            m_ld = S.op("sync", lambda y, csrc=csrc, fs=fs: y.dma_start(fs, csrc),
                        waits=(cast_prev[cb],), inc="d")
            m_cp = S.op("scalar", lambda a, fs=fs, bs=bs: a.copy(bs, fs),
                        waits=(m_ld,), inc="a")
            m_st = S.op("sync", lambda y, cdst=cdst, bs=bs: y.dma_start(cdst, bs),
                        waits=(m_cp,), inc="d")
            cast_prev[cb] = m_st
            cast_marks.append(m_st)

        xn_store = []
        xq_store = [None] * NT
        m_xnt_tile = [None] * NT
        xtile_free = [(), ()]
        prev_sq_reader = None
        ln_last = ((), ())
        for t in range(NT):
            b = t % 2
            xa = a_xf[:, b * D:(b + 1) * D]
            m_xl = S.op("sync", lambda y, t=t, xa=xa: y.dma_start(
                xa, x_in[t * 128:(t + 1) * 128, :]),
                waits=xtile_free[b], inc="d")
            c0 = a_st[:, 64 + 4 * b + 0:64 + 4 * b + 1]
            c1 = a_st[:, 64 + 4 * b + 1:64 + 4 * b + 2]
            c3 = a_st[:, 64 + 4 * b + 3:64 + 4 * b + 4]
            c2 = a_st[:, 2 * t:2 * t + 1]
            cr = a_st[:, 2 * t + 1:2 * t + 2]
            m1 = S.op("vector", lambda v, xa=xa, c0=c0: v.reduce_sum(
                c0, xa, axis=AX), waits=(m_xl,), inc="v")
            m2 = S.op("scalar", lambda a, xa=xa: a.activation(
                a_xsq[:, :], xa, AF.Square),
                waits=(m_xl, prev_sq_reader), inc="a")
            m3 = S.op("vector", lambda v, c1=c1: v.reduce_sum(
                c1, a_xsq[:, :], axis=AX), waits=(m2,), inc="v")
            m4 = S.op("vector", lambda v, c0=c0, c2=c2: v.tensor_single_scalar(
                c2, c0, 1.0 / D, OP.mult), waits=(m1,), inc="v")
            m5 = S.op("vector", lambda v, c2=c2, c3=c3: v.tensor_mul(
                c3, c2, c2), waits=(m4,), inc="v")
            m6a = S.op("vector", lambda v, c1=c1: v.tensor_single_scalar(
                c1, c1, 1.0 / D, OP.mult), waits=(m3,), inc="v")
            m6 = S.op("vector", lambda v, c0=c0, c1=c1, c3=c3: v.tensor_tensor(
                c0, c1, c3, op=OP.subtract), waits=(m6a, m5), inc="v")
            m7 = S.op("vector", lambda v, c0=c0: v.tensor_single_scalar(
                c0, c0, 1e-5, OP.add), waits=(m6,), inc="v")
            m8 = S.op("scalar", lambda a, c0=c0, c1=c1: a.activation(
                c1, c0, AF.Ln), waits=(m7,), inc="a")
            m9 = S.op("vector", lambda v, c1=c1, c3=c3: v.tensor_single_scalar(
                c3, c1, -0.5, OP.mult), waits=(m8,), inc="v")
            m10 = S.op("scalar", lambda a, c3=c3, cr=cr: a.activation(
                cr, c3, AF.Exp), waits=(m9,), inc="a")
            xnb = a_xnbf[:, b * D:(b + 1) * D]
            m11 = S.op("vector", lambda v, xa=xa, xnb=xnb, c2=c2, cr=cr:
                       v.tensor_scalar(xnb, xa, c2, cr,
                                       op0=OP.subtract, op1=OP.mult),
                       waits=(m10, xn_store[t - 2] if t >= 2 else None),
                       inc="v")
            m_xn_st = S.op("sync", lambda y, t=t, xnb=xnb: y.dma_start(
                xn_scr[t * 128:(t + 1) * 128, :], xnb), waits=(m11,), inc="d")
            xn_store.append(m_xn_st)
            prev_sq_reader = m3
            xtile_free[b] = (m11, m10)
            ln_last = (m11, m10)
            emit_cast_one(); emit_cast_one(); emit_cast_one()
            for c in range(DC):
                m_xnt_tile[t] = S.op("sync", lambda y, t=t, c=c: y.dma_start(
                    a_xnt[:, c * N + t * 128:c * N + (t + 1) * 128],
                    xn_scr[t * 128:(t + 1) * 128, c * 128:(c + 1) * 128],
                    transpose=True), waits=(m_xn_st,), inc="d")

        while cast_i[0] < len(cast_jobs):
            emit_cast_one()
        m_wq_cast = cast_marks[3]
        m_wk_cast = cast_marks[7]
        m_wv_cast = cast_marks[71]
        m_wv_cast_slab = lambda h: cast_marks[8 + 8 * h + 7]

        dq_state = {"qd": {"out": ln_last[0], "sgn": ln_last[1]},
                    "xfree": [ln_last, ln_last], "next": 0}

        def emit_deferred_xquant():
            t = dq_state["next"]
            if t >= NT:
                return
            dq_state["next"] = t + 1
            b = t % 2
            xa = a_xf[:, b * D:(b + 1) * D]
            m_xl = S.op("sync", lambda y, t=t, xa=xa: y.dma_start(
                xa, x_in[t * 128:(t + 1) * 128, :]),
                waits=dq_state["xfree"][b], inc="d")
            c2 = a_st[:, 2 * t:2 * t + 1]
            cr = a_st[:, 2 * t + 1:2 * t + 2]
            m_rec = S.op("vector", lambda v, xa=xa, c2=c2, cr=cr:
                         v.tensor_scalar(a_xnf[:, :], xa, c2, cr,
                                         op0=OP.subtract, op1=OP.mult),
                         waits=(m_xl, dq_state["qd"]["out"],
                                dq_state["qd"]["sgn"]), inc="v")
            if debug and t == 0:
                dbg_d0 = nc.dram_tensor("dbg_d0", [128, 2 * D], F32,
                                        kind="ExternalOutput").ap()
                S.op("sync", lambda y, xa=xa: y.dma_start(
                    dbg_d0[:, 0:D], xa), waits=(m_rec,), inc="d")
                S.op("sync", lambda y: y.dma_start(
                    dbg_d0[:, D:2 * D], a_xnf[:, :]), inc="d")
            xqb = a_xqbf[:, b * D:(b + 1) * D]
            qd = emit_quant(a_xnf[:, :], xqb, D, w_in=(m_rec,),
                            w_out=(xq_store[t - 2] if t >= 2 else None,))
            dq_state["qd"] = qd
            xq_store[t] = S.op("sync", lambda y, t=t, xqb=xqb: y.dma_start(
                xq_scr[t * 128:(t + 1) * 128, :], xqb),
                waits=(qd["out"],), inc="d")
            dq_state["xfree"][b] = (m_rec,)

        m_xnt = m_xnt_tile[NT - 1]

        m_wqt = m_wkt = None
        for c in range(DC):
            m_wqt = S.op("sync", lambda y, c=c: y.dma_start(
                a_wqt[:, c * 512:(c + 1) * 512],
                wq_bf[:, c * 128:(c + 1) * 128], transpose=True),
                waits=(m_wq_cast,), inc="d")
            m_wkt = S.op("sync", lambda y, c=c: y.dma_start(
                a_wkt[:, c * 512:(c + 1) * 512],
                wk_bf[:, c * 128:(c + 1) * 128], transpose=True),
                waits=(m_wk_cast,), inc="d")

        qk_store = {}
        maxabs_marks = []
        qk_pp = 0
        qk_evac_free = [None, None]
        qk_eb_free = [None, None]
        for which, wslab, scr, mtw in (("q", a_wqt, qt_scr, m_wqt),
                                       ("k", a_wkt, kt_scr, m_wkt)):
            for pc in range(4):
                for g in range(G):
                    bank = pO[qk_pp % 2]
                    eb = a_qke[:, (qk_pp % 2) * 512:(qk_pp % 2 + 1) * 512]
                    mm = None
                    for c in range(DC):
                        w0 = (mtw, m_xnt_tile[4 * g + 3],
                              qk_evac_free[qk_pp % 2]) if c == 0 else ()
                        mm = S.op("tensor", lambda T, bank=bank, wslab=wslab,
                                  c=c, pc=pc, g=g: T.matmul(
                                      bank[:, :],
                                      wslab[:, c * 512 + pc * 128:
                                            c * 512 + (pc + 1) * 128],
                                      a_xnt[:, c * N + g * GW:
                                            c * N + (g + 1) * GW],
                                      start=(c == 0), stop=(c == DC - 1)),
                                  waits=w0, inc="t" if c == DC - 1 else None)
                    m_ev = S.op("vector", lambda v, bank=bank, eb=eb:
                                v.tensor_copy(eb, bank[:, :]),
                                waits=(mm, qk_eb_free[qk_pp % 2]), inc="v")
                    m_st = S.op("sync", lambda y, scr=scr, pc=pc, g=g, eb=eb:
                                y.dma_start(scr[pc * 128:(pc + 1) * 128,
                                                g * GW:(g + 1) * GW], eb),
                                waits=(m_ev,), inc="d")
                    qk_store[(which, pc, g)] = m_st
                    qk_eb_free[qk_pp % 2] = m_st
                    col = pc + (0 if which == "q" else 4)
                    if g == 0:
                        mrm = S.op("vector", lambda v, eb=eb, col=col:
                                   v.tensor_reduce(
                                       smm[:, col:col + 1], eb, axis=AX,
                                       op=OP.max, apply_absolute_value=True),
                                   inc="v")
                    else:
                        m_r2 = S.op("vector", lambda v, eb=eb: v.tensor_reduce(
                            smm[:, 15:16], eb, axis=AX, op=OP.max,
                            apply_absolute_value=True), inc="v")
                        mrm = S.op("vector", lambda v, col=col: v.tensor_max(
                            smm[:, col:col + 1], smm[:, col:col + 1],
                            smm[:, 15:16]), waits=(m_r2,), inc="v")
                    qk_evac_free[qk_pp % 2] = mrm
                    if g == G - 1:
                        maxabs_marks.append(mrm)
                    qk_pp += 1
        m_qk_evac_last = qk_evac_free[(qk_pp - 1) % 2]

        m_sq = S.op("vector", lambda v: v.tensor_mul(
            smm[:, 16:24], smm[:, 0:8], smm[:, 0:8]),
            waits=tuple(maxabs_marks), inc="v")
        m_sqb = S.op("vector", lambda v: v.tensor_copy(
            cbf[:, 0:8], smm[:, 16:24]), waits=(m_sq,), inc="v")
        m_csum = None
        for idx in range(16):
            col = idx % 8
            lo = 0 if idx < 8 else 64
            m_csum = S.op("tensor", lambda T, idx=idx, col=col, lo=lo: T.matmul(
                pDen[0:1, idx:idx + 1], cbf[lo:lo + 64, col:col + 1],
                ones[lo:lo + 64, 0:1], start=(idx == 0), stop=(idx == 15),
                tile_position=(lo, 0), skip_group_check=True),
                waits=(m_sqb, m_ones), inc="t")
        m_cs = S.op("vector", lambda v: v.tensor_copy(
            smm[0:1, 24:40], pDen[0:1, 0:16]), waits=(m_csum,), inc="v")
        m_ch = []
        for h in range(H):
            pc2, par = h // 2, h % 2
            qc = 24 + par * 8 + pc2
            kc = 24 + par * 8 + 4 + pc2
            ma1 = S.op("vector", lambda v, h=h, qc=qc, kc=kc: v.tensor_add(
                smm[0:1, 40 + h:41 + h], smm[0:1, qc:qc + 1],
                smm[0:1, kc:kc + 1]), waits=(m_cs,), inc="v")
            ma2 = S.op("vector", lambda v, h=h: v.tensor_single_scalar(
                smm[0:1, 40 + h:41 + h], smm[0:1, 40 + h:41 + h], -0.5,
                OP.mult), waits=(ma1,), inc="v")
            m_ch.append(S.op("vector", lambda v, h=h: v.tensor_copy(
                cbf[0:1, 8 + h:9 + h], smm[0:1, 40 + h:41 + h]),
                waits=(ma2,), inc="v"))
        m_cneg = []
        prev_cp = m_cs
        for h in range(H):
            mM = S.op("tensor", lambda T, h=h: T.matmul(
                pDen[:, 0:1], ones[0:1, :], cbf[0:1, 8 + h:9 + h],
                start=True, stop=True, skip_group_check=True),
                waits=(m_ch[h], prev_cp), inc="t")
            mC = S.op("scalar", lambda a, h=h: a.copy(
                cneg[:, h:h + 1], pDen[:, 0:1]), waits=(mM,), inc="a")
            prev_cp = mC
            m_cneg.append(mC)

        v_spilled = {}
        wvt_free = [None, None]
        vout_store = None
        v_evac_last = None
        v_evac_waits = {}
        for h in range(H):
            wb = h % 2
            m_wvt = None
            for c in range(DC):
                m_wvt = S.op("sync", lambda y, h=h, c=c, wb=wb: y.dma_start(
                    a_wvt[:, wb * DC * D + c * D:wb * DC * D + (c + 1) * D],
                    wv_bf[h * D:(h + 1) * D, c * 128:(c + 1) * 128],
                    transpose=True),
                    waits=(m_wv_cast_slab(h), wvt_free[wb]), inc="d")
            mm = None
            for mc in range(NT):
                k8 = mc % 8
                pa = pS[2 * (mc % 2)]
                pb2 = pS[2 * (mc % 2) + 1]
                for c in range(DC):
                    w0 = ()
                    if c == 0:
                        w0 = (m_wvt, m_xnt_tile[mc], v_evac_waits.get(mc % 2))
                    S.op("tensor", lambda T, pa=pa, c=c, mc=mc, wb=wb: T.matmul(
                        pa[:, :],
                        a_xnt[:, c * N + mc * 128:c * N + (mc + 1) * 128],
                        a_wvt[:, wb * DC * D + c * D:wb * DC * D + c * D + 512],
                        start=(c == 0), stop=(c == DC - 1)), waits=w0)
                    mm = S.op("tensor", lambda T, pb2=pb2, c=c, mc=mc, wb=wb:
                              T.matmul(
                                  pb2[:, :],
                                  a_xnt[:, c * N + mc * 128:c * N + (mc + 1) * 128],
                                  a_wvt[:, wb * DC * D + c * D + 512:
                                        wb * DC * D + (c + 1) * D],
                                  start=(c == 0), stop=(c == DC - 1)),
                              inc="t" if c == DC - 1 else None, no_ldw=True)
                d0 = a_vout[:, 0 * 4096 + k8 * 512:0 * 4096 + (k8 + 1) * 512]
                d1 = a_vout[:, 1 * 4096 + k8 * 512:1 * 4096 + (k8 + 1) * 512]
                S.op("scalar", lambda a, pa=pa, d0=d0: a.copy(
                    d0, pa[:, :]),
                    waits=(mm, vout_store if k8 == 0 else None), inc="a")
                m_ev = S.op("scalar", lambda a, pb2=pb2, d1=d1: a.copy(
                    d1, pb2[:, :]), inc="a")
                v_evac_waits[mc % 2] = m_ev
                v_evac_last = m_ev
                if k8 == 7:
                    blk8 = mc // 8
                    for half in range(2):
                        vout_store = S.op(
                            "sync", lambda y, h=h, half=half, blk8=blk8:
                            y.dma_start(
                                v_scr[h, half, :, blk8 * 8:(blk8 + 1) * 8, :],
                                a_vout[:, half * 4096:(half + 1) * 4096]
                                .rearrange("p (k d) -> p k d", d=512)),
                            waits=(m_ev,), inc="d")
                    v_spilled[(h, blk8)] = vout_store
            wvt_free[wb] = mm
            for _ in range(4):
                emit_deferred_xquant()
        while dq_state["next"] < NT:
            emit_deferred_xquant()

        if debug:
            dbg_c = nc.dram_tensor("dbg_c", [128, 64 + 8], F32,
                                   kind="ExternalOutput").ap()
            S.op("sync", lambda y: y.dma_start(dbg_c[:, 0:8], cneg[:, :]),
                 waits=(m_cneg[-1],), inc="d")
            S.op("sync", lambda y: y.dma_start(dbg_c[:, 8:72], smm[:, :]),
                 inc="d")
            dbg_a = nc.dram_tensor("dbg_a", [128, 80], F32,
                                   kind="ExternalOutput").ap()
            dbg_xnf = nc.dram_tensor("dbg_xnf", [128, D], F32,
                                     kind="ExternalOutput").ap()
            dbg_xqb = nc.dram_tensor("dbg_xqb", [128, 2 * D], BF16,
                                     kind="ExternalOutput").ap()
            S.op("sync", lambda y: y.dma_start(dbg_a, a_st[:, :]),
                 waits=(dq_state["qd"]["out"], dq_state["qd"]["sgn"]), inc="d")
            S.op("sync", lambda y: y.dma_start(dbg_xnf, a_xnf[:, :]), inc="d")
            S.op("sync", lambda y: y.dma_start(dbg_xqb, a_xqbf[:, :]), inc="d")
            dbg_qp = nc.dram_tensor("dbg_qp", [128, 5 * 1024 + 3 * 176], F32,
                                    kind="ExternalOutput").ap()
            for qi in range(5):
                S.op("sync", lambda y, qi=qi: y.dma_start(
                    dbg_qp[:, qi * 1024:(qi + 1) * 1024], qp[qi][:, :]), inc="d")
            S.op("sync", lambda y: y.dma_start(
                dbg_qp[:, 5120:5296], thrp[:, :]), inc="d")
            S.op("sync", lambda y: y.dma_start(
                dbg_qp[:, 5296:5472], thr6[:, :]), inc="d")
            S.op("sync", lambda y: y.dma_start(
                dbg_qp[:, 5472:5648], tks[0][:, :]), inc="d")

        S.fence()
        stA.close()
        stB = ExitStack()
        e = stB.enter_context
        b_kt = e(nc.sbuf_tensor("b_kt", [128, 2 * N], BF16))
        b_qt = e(nc.sbuf_tensor("b_qt", [128, 2 * 512], BF16))
        b_et = e(nc.sbuf_tensor("b_et", [128, 2 * NT * 512], BF16))
        b_vb = e(nc.sbuf_tensor("b_vb", [128, 2 * NT * 512], BF16))
        b_oacc = e(nc.sbuf_tensor("b_oacc", [128, 4 * D], F32))
        b_rden = e(nc.sbuf_tensor("b_rden", [128, 8], F32))
        b_wpf = e(nc.sbuf_tensor("b_wpf", [128, D], F32))
        b_wpb = e(nc.sbuf_tensor("b_wpb", [128, D], BF16))

        wprep_jobs = []
        for s in range(22):
            wprep_jobs.append((wg_in[s * 128:(s + 1) * 128, :],
                               wgq_scr[s * 128:(s + 1) * 128, :]))
        for s in range(22):
            wprep_jobs.append((wu_in[s * 128:(s + 1) * 128, :],
                               wuq_scr[s * 128:(s + 1) * 128, :]))
        for s in range(8):
            for sec in range(3):
                w = 1024 if sec < 2 else 768
                wprep_jobs.append((wd_in[s * 128:(s + 1) * 128,
                                         sec * 1024:sec * 1024 + w],
                                   wdq_scr[s * 128:(s + 1) * 128,
                                           sec * 1024:sec * 1024 + w]))
        wprep_i = [0]
        wprep_st = [None]
        wprep_qd = [{"out": None, "sgn": None}]

        def emit_wprep_one():
            if wprep_i[0] >= len(wprep_jobs):
                return
            src, dst = wprep_jobs[wprep_i[0]]
            W = src.shape[1]
            wprep_i[0] += 1
            m_ld = S.op("sync", lambda y, src=src, W=W: y.dma_start(
                b_wpf[:, 0:W], src),
                waits=(wprep_qd[0]["out"], wprep_qd[0]["sgn"]), inc="d")
            qd = emit_quant(b_wpf[:, 0:W], b_wpb[:, 0:W], W,
                            w_in=(m_ld,), w_out=(wprep_st[0],))
            wprep_qd[0] = qd
            wprep_st[0] = S.op("sync", lambda y, dst=dst, W=W: y.dma_start(
                dst, b_wpb[:, 0:W]), waits=(qd["out"],), inc="d")

        kt_free = [m_qk_evac_last, m_qk_evac_last]
        vb_free = [v_evac_last, v_evac_last]
        pO_free = {0: m_qk_evac_last, 1: m_qk_evac_last, 2: None}
        pden_free = m_cneg[-1]
        et_consumed = [v_evac_last, v_evac_last]
        oacc_stored = None
        o_rc = [0]
        exp_marks = None

        for g in range(G):
            for p in range(4):
                pb = p % 2
                m_kt = S.op("sync", lambda y, p=p, pb=pb: y.dma_start(
                    b_kt[:, pb * N:(pb + 1) * N],
                    kt_scr[p * 128:(p + 1) * 128, :]),
                    waits=(qk_store[("k", p, G - 1)], kt_free[pb]), inc="d")
                m_qt = S.op("sync", lambda y, p=p, g=g, pb=pb: y.dma_start(
                    b_qt[:, pb * 512:(pb + 1) * 512],
                    qt_scr[p * 128:(p + 1) * 128, g * GW:(g + 1) * GW]),
                    waits=(qk_store[("q", p, g)],), inc="d")
                exp_marks = {0: {}, 1: {}}
                m_s = None
                for mc in range(NT):
                    pA = pS[mc % 2]
                    pB = pS[2 + mc % 2]
                    wA = [m_kt, m_qt]
                    if mc >= 2:
                        wA += [exp_marks[0][mc - 2], exp_marks[1][mc - 2]]
                    else:
                        wA += [et_consumed[0], et_consumed[1]]
                    S.op("tensor", lambda T, pA=pA, pb=pb, mc=mc: T.matmul(
                        pA[:, :],
                        b_kt[0:64, pb * N + mc * 128:pb * N + (mc + 1) * 128],
                        b_qt[0:64, pb * 512:(pb + 1) * 512],
                        start=True, stop=True), waits=tuple(wA))
                    m_s = S.op("tensor", lambda T, pB=pB, pb=pb, mc=mc: T.matmul(
                        pB[:, :],
                        b_kt[64:128, pb * N + mc * 128:pb * N + (mc + 1) * 128],
                        b_qt[64:128, pb * 512:(pb + 1) * 512],
                        start=True, stop=True, tile_position=(64, 0)), inc="t")
                    for hh in range(2):
                        h = 2 * p + hh
                        psrc = pA if hh == 0 else pB
                        exp_marks[hh][mc] = S.op(
                            "scalar", lambda a, hh=hh, mc=mc, psrc=psrc, h=h:
                            a.activation(
                                b_et[:, hh * NT * 512 + mc * 512:
                                     hh * NT * 512 + (mc + 1) * 512],
                                psrc[:, :], AF.Exp, bias=cneg[:, h:h + 1]),
                            waits=(m_s, m_cneg[h]), inc="a")
                kt_free[pb] = m_s
                for hh in range(2):
                    h = 2 * p + hh
                    m_exp_last = exp_marks[hh][NT - 1]
                    mm_last = None
                    m_vl = {}
                    m_rc_sp = {}
                    for sp in range(2):
                        for half in range(2):
                            vbuf = half
                            if sp == 0:
                                m_vl[half] = S.op(
                                    "sync", lambda y, h=h, half=half, vbuf=vbuf:
                                    y.dma_start(
                                        b_vb[:, vbuf * NT * 512:
                                             (vbuf + 1) * NT * 512],
                                        v_scr[h, half, :, :, :].rearrange(
                                            "p k d -> p (k d)")),
                                    waits=(v_spilled[(h, 3)], vb_free[vbuf]),
                                    inc="d")
                            rc = o_rc[0]
                            o_rc[0] += 1
                            banks = (pO[(2 * rc) % 3], pO[(2 * rc + 1) % 3])
                            for mc in range(NT):
                                for si, sub in enumerate((2 * sp, 2 * sp + 1)):
                                    bk = banks[si]
                                    w0 = ()
                                    if mc == 0:
                                        w0 = (m_exp_last, m_vl[half],
                                              pO_free[(2 * rc + si) % 3])
                                        if half == 0 and si == 0:
                                            w0 = w0 + ((pden_free,) if sp == 0
                                                       else (m_rc_sp[0],))
                                    is_last = (mc == NT - 1 and si == 1)
                                    mm_last = S.op(
                                        "tensor",
                                        lambda T, bk=bk, hh=hh, mc=mc, sub=sub,
                                        vbuf=vbuf: T.matmul(
                                            bk[:, :],
                                            b_et[:, hh * NT * 512 + mc * 512
                                                 + sub * 128:
                                                 hh * NT * 512 + mc * 512
                                                 + (sub + 1) * 128],
                                            b_vb[:, vbuf * NT * 512 + mc * 512:
                                                 vbuf * NT * 512
                                                 + (mc + 1) * 512],
                                            start=(mc == 0),
                                            stop=(mc == NT - 1)),
                                        waits=w0,
                                        inc="t" if (is_last and half == 1)
                                        else None)
                                    if half == 0:
                                        m_rid = S.op(
                                            "tensor", lambda T, hh=hh, mc=mc,
                                            sub=sub, sp=sp: T.matmul(
                                                pDen[:, sub:sub + 1],
                                                b_et[:, hh * NT * 512
                                                     + mc * 512 + sub * 128:
                                                     hh * NT * 512 + mc * 512
                                                     + (sub + 1) * 128],
                                                ones[:, 0:1],
                                                start=(mc == 0 and sub == 0
                                                       and sp == 0),
                                                stop=(mc == NT - 1 and sub == 3),
                                                skip_group_check=True),
                                            inc="t" if is_last else None,
                                            no_ldw=True)
                                        if is_last:
                                            mm_last = m_rid
                            if half == 0:
                                m_rc_sp[sp] = S.op(
                                    "vector", lambda v, sp=sp: v.reciprocal(
                                        b_rden[:, 2 * sp:2 * sp + 2],
                                        pDen[:, 2 * sp:2 * sp + 2]),
                                    waits=(mm_last,), inc="v")
                                if sp == 1:
                                    pden_free = m_rc_sp[1]
                            for si, sub in enumerate((2 * sp, 2 * sp + 1)):
                                bk = banks[si]
                                dst = b_oacc[:, sub * D + half * 512:
                                             sub * D + half * 512 + 512]
                                w0 = (mm_last, m_rc_sp[sp], oacc_stored)
                                if h == 0:
                                    m_ev = S.op(
                                        "vector", lambda v, bk=bk, dst=dst,
                                        sub=sub: v.tensor_scalar(
                                            dst, bk[:, :],
                                            b_rden[:, sub:sub + 1], None,
                                            op0=OP.mult),
                                        waits=w0, inc="v")
                                else:
                                    m_ev = S.op(
                                        "vector", lambda v, bk=bk, dst=dst,
                                        sub=sub: v.scalar_tensor_tensor(
                                            dst, bk[:, :],
                                            b_rden[:, sub:sub + 1], dst,
                                            op0=OP.mult, op1=OP.add),
                                        waits=w0, inc="v")
                                pO_free[(2 * rc + si) % 3] = m_ev
                            if sp == 1:
                                vb_free[vbuf] = mm_last
                    et_consumed[hh] = mm_last
                emit_wprep_one()
                emit_wprep_one()
            last_ev = pO_free[(2 * (o_rc[0] - 1) + 1) % 3]
            for sub in range(4):
                oacc_stored = S.op("sync", lambda y, g=g, sub=sub: y.dma_start(
                    oacc_scr[g * GW + sub * 128:g * GW + (sub + 1) * 128, :],
                    b_oacc[:, sub * D:(sub + 1) * D]),
                    waits=(last_ev,), inc="d")
        while wprep_i[0] < len(wprep_jobs):
            emit_wprep_one()
        m_wprep_done = wprep_st[0]
        b_last_exp = [exp_marks[0][NT - 1], exp_marks[1][NT - 1]]

        S.fence()
        stB.close()
        stC = ExitStack()
        e = stC.enter_context
        c_wg = e(nc.sbuf_tensor("c_wg", [128, DC * I], BF16))
        c_wu = e(nc.sbuf_tensor("c_wu", [128, DC * I], BF16))
        c_wdt = e(nc.sbuf_tensor("c_wdt", [128, IC * D], BF16))
        c_xqt = e(nc.sbuf_tensor("c_xqt", [128, 2 * DC * 128], BF16))
        c_silu = e(nc.sbuf_tensor("c_silu", [128, 512], F32))
        c_hf = e(nc.sbuf_tensor("c_hf", [128, 512], F32))
        c_hq = e(nc.sbuf_tensor("c_hq", [128, 2 * I], BF16))
        c_hqt = e(nc.sbuf_tensor("c_hqt", [128, 2 * IC * 128], BF16))
        c_xo = e(nc.sbuf_tensor("c_xo", [128, 2 * D], F32))
        c_out = e(nc.sbuf_tensor("c_out", [128, D], F32))

        m_cw = None
        for c in range(DC):
            m_cw = S.op("sync", lambda y, c=c: y.dma_start(
                c_wg[:, c * I:(c + 1) * I],
                wgq_scr[:, c * 128:(c + 1) * 128], transpose=True),
                waits=(m_wprep_done,), inc="d")
            m_cw = S.op("sync", lambda y, c=c: y.dma_start(
                c_wu[:, c * I:(c + 1) * I],
                wuq_scr[:, c * 128:(c + 1) * 128], transpose=True), inc="d")
        for ic in range(IC):
            m_cw = S.op("sync", lambda y, ic=ic: y.dma_start(
                c_wdt[:, ic * D:(ic + 1) * D],
                wdq_scr[:, ic * 128:(ic + 1) * 128], transpose=True), inc="d")

        SEGS = [(0, 512), (512, 1024), (1024, 1536), (1536, 2048),
                (2048, 2560), (2560, 2816)]
        xqt_free = [b_last_exp[0], b_last_exp[1]]
        xo_free = None
        hq_seg_store = [[None] * len([1]*6) for _ in range(2)]
        hqt_free = [None, None]
        out_stored = None
        seg_ps_free = {0: b_last_exp[0], 1: b_last_exp[0],
                       2: b_last_exp[1], 3: b_last_exp[1]}
        down_free = {0: None, 1: None}
        prev_cq = [{"out": None, "sgn": None}]
        for nb in range(NT):
            b = nb % 2
            m_xqt = None
            for c in range(DC):
                m_xqt = S.op("sync", lambda y, nb=nb, c=c, b=b: y.dma_start(
                    c_xqt[:, b * DC * 128 + c * 128:b * DC * 128 + (c + 1) * 128],
                    xq_scr[nb * 128:(nb + 1) * 128, c * 128:(c + 1) * 128],
                    transpose=True),
                    waits=(xq_store[nb], xqt_free[b]), inc="d")
            m_x = S.op("sync", lambda y, nb=nb: y.dma_start(
                c_xo[:, 0:D], x_in[nb * 128:(nb + 1) * 128, :]),
                waits=(xo_free,), inc="d")
            m_o = S.op("sync", lambda y, nb=nb: y.dma_start(
                c_xo[:, D:2 * D], oacc_scr[nb * 128:(nb + 1) * 128, :]),
                waits=(oacc_stored,), inc="d")
            m_seg_out = []
            mm_u = None
            m_hqt_ic = [None] * IC
            for si, (s0, s1) in enumerate(SEGS):
                W = s1 - s0
                pg = pS[2 * (si % 2)]
                pu = pS[2 * (si % 2) + 1]
                for c in range(DC):
                    w0 = ()
                    if c == 0:
                        w0 = (m_xqt, m_cw, seg_ps_free[2 * (si % 2)],
                              seg_ps_free[2 * (si % 2) + 1])
                    S.op("tensor", lambda T, pg=pg, c=c, s0=s0, s1=s1, b=b:
                         T.matmul(pg[:, 0:s1 - s0],
                                  c_xqt[:, b * DC * 128 + c * 128:
                                        b * DC * 128 + (c + 1) * 128],
                                  c_wg[:, c * I + s0:c * I + s1],
                                  start=(c == 0), stop=(c == DC - 1)),
                         waits=w0)
                    mm_u = S.op("tensor", lambda T, pu=pu, c=c, s0=s0, s1=s1,
                                b=b: T.matmul(
                                    pu[:, 0:s1 - s0],
                                    c_xqt[:, b * DC * 128 + c * 128:
                                          b * DC * 128 + (c + 1) * 128],
                                    c_wu[:, c * I + s0:c * I + s1],
                                    start=(c == 0), stop=(c == DC - 1)),
                                inc="t" if c == DC - 1 else None, no_ldw=True)
                m_si = S.op("scalar", lambda a, pg=pg, W=W: a.activation(
                    c_silu[:, 0:W], pg[:, 0:W], AF.Silu),
                    waits=(mm_u, prev_cq[0]["out"]), inc="a")
                m_h = S.op("vector", lambda v, pu=pu, W=W: v.tensor_mul(
                    c_hf[:, 0:W], c_silu[:, 0:W], pu[:, 0:W]),
                    waits=(m_si, prev_cq[0]["out"]), inc="v")
                seg_ps_free[2 * (si % 2)] = m_h
                seg_ps_free[2 * (si % 2) + 1] = m_h
                qd = emit_quant(c_hf[:, 0:W], c_hq[:, b * I + s0:b * I + s1],
                                W, w_in=(m_h,),
                                w_out=(hq_seg_store[b][si],))
                prev_cq[0] = qd
                m_seg_out.append(qd["out"])
                m_sst = S.op("sync", lambda y, nb=nb, s0=s0, s1=s1, b=b:
                             y.dma_start(
                                 hq_scr[nb * 128:(nb + 1) * 128, s0:s1],
                                 c_hq[:, b * I + s0:b * I + s1]),
                             waits=(qd["out"],), inc="d")
                hq_seg_store[b][si] = m_sst
                ic0, ic1 = s0 // 128, s1 // 128
                for ic in range(ic0, ic1):
                    m_hqt_ic[ic] = S.op(
                        "sync", lambda y, nb=nb, ic=ic, b=b: y.dma_start(
                            c_hqt[:, b * IC * 128 + ic * 128:
                                  b * IC * 128 + (ic + 1) * 128],
                            hq_scr[nb * 128:(nb + 1) * 128,
                                   ic * 128:(ic + 1) * 128],
                            transpose=True),
                        waits=(m_sst, hqt_free[b]), inc="d")
            mm_d = None
            for ic in range(IC):
                w0 = (m_hqt_ic[ic],)
                if ic == 0:
                    w0 = w0 + (down_free[0], down_free[1])
                S.op("tensor", lambda T, ic=ic, b=b: T.matmul(
                    pO[0][:, :],
                    c_hqt[:, b * IC * 128 + ic * 128:b * IC * 128 + (ic + 1) * 128],
                    c_wdt[:, ic * D:ic * D + 512],
                    start=(ic == 0), stop=(ic == IC - 1)), waits=w0)
                mm_d = S.op("tensor", lambda T, ic=ic, b=b: T.matmul(
                    pO[1][:, :],
                    c_hqt[:, b * IC * 128 + ic * 128:b * IC * 128 + (ic + 1) * 128],
                    c_wdt[:, ic * D + 512:(ic + 1) * D],
                    start=(ic == 0), stop=(ic == IC - 1)),
                    inc="t" if ic == IC - 1 else None, no_ldw=True)
            hqt_free[b] = mm_d
            m_xo2 = S.op("vector", lambda v: v.tensor_add(
                c_xo[:, 0:D], c_xo[:, 0:D], c_xo[:, D:2 * D]),
                waits=(m_x, m_o), inc="v")
            mo0 = S.op("vector", lambda v: v.tensor_add(
                c_out[:, 0:512], pO[0][:, :], c_xo[:, 0:512]),
                waits=(mm_d, m_xo2, out_stored), inc="v")
            mo1 = S.op("vector", lambda v: v.tensor_add(
                c_out[:, 512:D], pO[1][:, :], c_xo[:, 512:D]), inc="v")
            down_free[0] = mo0
            down_free[1] = mo1
            xo_free = mo1
            xqt_free[b] = mm_u
            out_stored = S.op("sync", lambda y, nb=nb: y.dma_start(
                out_t[nb * 128:(nb + 1) * 128, :], c_out[:, :]),
                waits=(mo1,), inc="d")

        S.op("sync", lambda y: None, waits=(("d", S.cnt["d"]),))
        stC.close()

        _EMIT_GUARD = True
        with (nc.semaphore("sem_d") as hd,
              nc.semaphore("sem_t") as ht,
              nc.semaphore("sem_v") as hv,
              nc.semaphore("sem_a") as ha,
              nc.semaphore("sem_e") as he,
              nc.Block() as blk):
            handles = {"d": hd, "t": ht, "v": hv, "a": ha, "e": he}

            @blk.sync
            def _(eng):
                S.emit("sync", eng, handles)

            @blk.tensor
            def _(eng):
                S.emit("tensor", eng, handles)

            @blk.vector
            def _(eng):
                S.emit("vector", eng, handles)

            @blk.scalar
            def _(eng):
                S.emit("scalar", eng, handles)

            @blk.gpsimd
            def _(eng):
                S.emit("gpsimd", eng, handles)

    import dataclasses
    names = getattr(S, "no_ldw_names", set())
    if names:
        for f in nc.m.functions:
            for bb in f.blocks:
                insts = bb.instructions
                for i, ins in enumerate(insts):
                    if isinstance(ins, mybir.InstMatmult) and ins.name in names:
                        insts[i] = dataclasses.replace(ins, ldweights=False)
    return nc


_CACHE = {}


def kernel(L0_attn_x, L0_attn_Wq, L0_attn_Wk, L0_attn_Wv,
           L0_ffn_w_gate, L0_ffn_w_up, L0_ffn_w_down):
    if "nc" not in _CACHE:
        _CACHE["nc"] = build_program()
    nc = _CACHE["nc"]
    x = np.ascontiguousarray(np.asarray(L0_attn_x, dtype=np.float32))
    in_common = dict(
        wq=np.ascontiguousarray(np.asarray(L0_attn_Wq, dtype=np.float32)),
        wk=np.ascontiguousarray(np.asarray(L0_attn_Wk, dtype=np.float32)),
        wv=np.ascontiguousarray(np.asarray(L0_attn_Wv, dtype=np.float32)),
        wg=np.ascontiguousarray(np.asarray(L0_ffn_w_gate, dtype=np.float32)),
        wu=np.ascontiguousarray(np.asarray(L0_ffn_w_up, dtype=np.float32)),
        wd=np.ascontiguousarray(np.asarray(L0_ffn_w_down, dtype=np.float32)))
    in_maps = [dict(x=np.ascontiguousarray(x[i]), **in_common)
               for i in range(8)]
    res = run_bass_kernel_spmd(nc, in_maps, core_ids=list(range(8)))
    return np.stack([np.asarray(r["out"], dtype=np.float32)
                     for r in res.results], axis=0)


# version 10
# speedup vs baseline: 1.0174x; 1.0174x over previous
import numpy as np
import concourse.bass as bass
import concourse.mybir as mybir
from concourse.bass_utils import run_bass_kernel_spmd

F32 = mybir.dt.float32
BF16 = mybir.dt.bfloat16
AF = mybir.ActivationFunctionType
OP = mybir.AluOpType
AX = mybir.AxisListType.X

N = 4096
D = 1024
H = 8
I = 2816
NT = N // 128
DC = D // 128
G = 8
GW = 512
IC = I // 128

MIDS = [0.25, 0.75, 1.25, 1.75, 2.5, 3.5, 5.0]

ENGS = ("sync", "tensor", "vector", "scalar", "gpsimd")


class Sched:

    def __init__(self):
        self.ops = {e: [] for e in ENGS}
        self.cnt = {s: 0 for s in "dtvaeg"}
        self.pending_fence = {}

    def op(self, eng, fn, waits=(), inc=None, no_ldw=False):
        waits = [w for w in waits if w is not None]
        if eng in self.pending_fence:
            waits = list(self.pending_fence.pop(eng)) + waits
        mark = None
        if inc is not None:
            self.cnt[inc] += 16 if inc in "de" else 1
            mark = (inc, self.cnt[inc])
        self.ops[eng].append((fn, tuple(waits), inc, mark, no_ldw))
        return mark

    def fence(self):
        snap = tuple((s, v) for s, v in self.cnt.items() if v > 0)
        for e in ENGS:
            self.pending_fence[e] = snap

    def emit(self, eng_name, eng, handles):
        guaranteed = {}
        self.no_ldw_names = getattr(self, "no_ldw_names", set())
        for fn, waits, inc, mark, no_ldw in self.ops[eng_name]:
            for (sem, val) in waits:
                if guaranteed.get(sem, 0) < val:
                    eng.wait_ge(handles[sem], val)
                    guaranteed[sem] = val
            inst = fn(eng)
            if inc is not None:
                inst.then_inc(handles[inc], 16 if inc in "de" else 1)
            if no_ldw:
                self.no_ldw_names.add(inst.ins.name)


def build_program(debug=False):
    nc = bass.Bass("TRN2", target_bir_lowering=False)
    IK = "ExternalOutput" if debug else "Internal"

    x_in = nc.dram_tensor("x", [N, D], F32, kind="ExternalInput").ap()
    wq_in = nc.dram_tensor("wq", [512, D], F32, kind="ExternalInput").ap()
    wk_in = nc.dram_tensor("wk", [512, D], F32, kind="ExternalInput").ap()
    wv_in = nc.dram_tensor("wv", [H * D, D], F32, kind="ExternalInput").ap()
    wg_in = nc.dram_tensor("wg", [I, D], F32, kind="ExternalInput").ap()
    wu_in = nc.dram_tensor("wu", [I, D], F32, kind="ExternalInput").ap()
    wd_in = nc.dram_tensor("wd", [D, I], F32, kind="ExternalInput").ap()
    out_t = nc.dram_tensor("out", [N, D], F32, kind="ExternalOutput").ap()

    xn_scr = nc.dram_tensor("xn_scr", [N, D], BF16, kind=IK).ap()
    xq_scr = nc.dram_tensor("xq_scr", [N, D], BF16, kind=IK).ap()
    qt_scr = nc.dram_tensor("qt_scr", [512, N], BF16, kind=IK).ap()
    kt_scr = nc.dram_tensor("kt_scr", [512, N], BF16, kind=IK).ap()
    wq_bf = nc.dram_tensor("wq_bf", [512, D], BF16, kind=IK).ap()
    wk_bf = nc.dram_tensor("wk_bf", [512, D], BF16, kind=IK).ap()
    wv_bf = nc.dram_tensor("wv_bf", [H * D, D], BF16, kind=IK).ap()
    v_scr = nc.dram_tensor("v_scr", [H, 2, 128, NT, 512], BF16, kind=IK).ap()
    oacc_scr = nc.dram_tensor("oacc_scr", [N, D], F32, kind=IK).ap()
    wgq_scr = nc.dram_tensor("wgq_scr", [I, D], BF16, kind=IK).ap()
    wuq_scr = nc.dram_tensor("wuq_scr", [I, D], BF16, kind=IK).ap()
    wdq_scr = nc.dram_tensor("wdq_scr", [D, I], BF16, kind=IK).ap()
    hq_scr = nc.dram_tensor("hq_scr", [N, I], BF16, kind=IK).ap()

    S = Sched()
    from contextlib import ExitStack
    with ExitStack() as st:
        e = st.enter_context
        ones = e(nc.sbuf_tensor("ones", [128, 128], BF16))
        cneg = e(nc.sbuf_tensor("cneg", [128, 8], F32))
        qp = [e(nc.sbuf_tensor(f"qp{i}", [128, 1024], F32)) for i in range(5)]
        thrp = e(nc.sbuf_tensor("thrp", [128, 176], F32))
        thr6 = e(nc.sbuf_tensor("thr6", [128, 176], F32))
        tks = [e(nc.sbuf_tensor(f"tk{i}", [128, 176], F32)) for i in range(7)]
        smm = e(nc.sbuf_tensor("smm", [128, 64], F32))
        cbf = e(nc.sbuf_tensor("cbf", [128, 16], BF16))

        pS = [e(nc.psum_tensor(f"pS{i}", [128, 512], F32)) for i in range(4)]
        pO = [e(nc.psum_tensor(f"pO{i}", [128, 512], F32)) for i in range(3)]
        pDen = e(nc.psum_tensor("pDen", [128, 16], F32))

        def emit_quant(in_ap, out_ap, W, w_in=(), w_out=()):
            B = W // 16
            v3 = lambda t: t[:, 0:W].rearrange("p (b i) -> p b i", i=16)
            b3 = lambda t: t[:, 0:B].rearrange("p (b o) -> p b o", o=1)
            bb = lambda t: b3(t).broadcast_to((128, B, 16))
            in3 = in_ap.rearrange("p (b i) -> p b i", i=16)
            m_thr = S.op("vector", lambda v: v.tensor_reduce(
                b3(thrp), in3, axis=AX, op=OP.max, apply_absolute_value=True),
                waits=tuple(w_in), inc="v")
            m_t6 = S.op("vector", lambda v: v.tensor_single_scalar(
                thr6[:, 0:B], thrp[:, 0:B], float(np.float32(6e-12)), OP.max),
                waits=(m_thr,), inc="v")
            m_tk0 = None
            for k in range(7):
                m_tk0 = S.op("vector", lambda v, k=k: v.tensor_single_scalar(
                    tks[k][:, 0:B], thr6[:, 0:B], MIDS[k], OP.mult),
                    waits=(m_t6,), inc="v")
            m_abs = S.op("scalar", lambda a: a.activation(
                qp[0][:, 0:W], in_ap, AF.Abs),
                waits=tuple(w_in) + (m_thr,), inc="a")
            m_sgn = S.op("scalar", lambda a: a.sign(
                qp[1][:, 0:W], in_ap), inc="a")
            S.op("vector", lambda v: v.tensor_single_scalar(
                qp[0][:, 0:W], qp[0][:, 0:W], 6.0, OP.mult),
                waits=(m_abs,), inc="v")

            def mask(k, dst):
                S.op("vector", lambda v, k=k, dst=dst: v.tensor_tensor(
                    v3(dst), v3(qp[0]), bb(tks[k]), op=OP.is_gt),
                    waits=(m_tk0,), inc="v")
            mask(0, qp[2]); mask(1, qp[3])
            S.op("vector", lambda v: v.tensor_add(
                qp[2][:, 0:W], qp[2][:, 0:W], qp[3][:, 0:W]), inc="v")
            mask(2, qp[3]); mask(3, qp[4])
            S.op("vector", lambda v: v.tensor_add(
                qp[3][:, 0:W], qp[3][:, 0:W], qp[4][:, 0:W]), inc="v")
            S.op("vector", lambda v: v.tensor_add(
                qp[2][:, 0:W], qp[2][:, 0:W], qp[3][:, 0:W]), inc="v")
            mask(4, qp[3]); mask(5, qp[4])
            S.op("vector", lambda v: v.tensor_add(
                qp[3][:, 0:W], qp[3][:, 0:W], qp[4][:, 0:W]), inc="v")
            S.op("vector", lambda v: v.scalar_tensor_tensor(
                qp[2][:, 0:W], qp[2][:, 0:W], 0.5, qp[3][:, 0:W],
                op0=OP.mult, op1=OP.add), inc="v")
            mask(6, qp[3])
            S.op("vector", lambda v: v.scalar_tensor_tensor(
                qp[2][:, 0:W], qp[3][:, 0:W], 2.0, qp[2][:, 0:W],
                op0=OP.mult, op1=OP.add), inc="v")
            S.op("vector", lambda v: v.scalar_tensor_tensor(
                v3(qp[1]), v3(qp[1]), float(np.float32(1.0 / 6.0)), bb(thr6),
                op0=OP.mult, op1=OP.mult), waits=(m_sgn,), inc="v")
            m_out = S.op("vector", lambda v: v.tensor_mul(
                out_ap, qp[2][:, 0:W], qp[1][:, 0:W]),
                waits=tuple(w_out), inc="v")
            return {"out": m_out, "sgn": m_sgn}

        m_ones = S.op("vector", lambda v: v.memset(ones[:, :], 1.0), inc="v")

        stA = ExitStack()
        e = stA.enter_context
        a_xf = e(nc.sbuf_tensor("a_xf", [128, 2 * D], F32))
        a_xsq = e(nc.sbuf_tensor("a_xsq", [128, D], F32))
        a_st = e(nc.sbuf_tensor("a_st", [128, 80], F32))
        a_xnbf = e(nc.sbuf_tensor("a_xnbf", [128, 2 * D], BF16))
        a_xnf = e(nc.sbuf_tensor("a_xnf", [128, D], F32))
        a_xqbf = e(nc.sbuf_tensor("a_xqbf", [128, 2 * D], BF16))
        a_xnt = e(nc.sbuf_tensor("a_xnt", [128, DC * N], BF16))
        a_wvt = e(nc.sbuf_tensor("a_wvt", [128, 2 * DC * D], BF16))
        a_vout = e(nc.sbuf_tensor("a_vout", [128, 2 * 8 * 512], BF16))
        a_wqt = e(nc.sbuf_tensor("a_wqt", [128, DC * 512], BF16))
        a_wkt = e(nc.sbuf_tensor("a_wkt", [128, DC * 512], BF16))
        a_qke = e(nc.sbuf_tensor("a_qke", [128, 2 * 512], BF16))
        a_wcf = e(nc.sbuf_tensor("a_wcf", [128, 4 * D], F32))
        a_wcb = e(nc.sbuf_tensor("a_wcb", [128, 4 * D], BF16))

        cast_jobs = [(wq_in[s * 128:(s + 1) * 128, :],
                      wq_bf[s * 128:(s + 1) * 128, :]) for s in range(4)]
        cast_jobs += [(wk_in[s * 128:(s + 1) * 128, :],
                       wk_bf[s * 128:(s + 1) * 128, :]) for s in range(4)]
        cast_jobs += [(wv_in[s * 128:(s + 1) * 128, :],
                       wv_bf[s * 128:(s + 1) * 128, :]) for s in range(64)]
        cast_prev = [None] * 4
        cast_marks = []
        cast_i = [0]

        def emit_cast_one():
            j = cast_i[0]
            if j >= len(cast_jobs):
                return
            cast_i[0] = j + 1
            csrc, cdst = cast_jobs[j]
            cb = j % 4
            fs = a_wcf[:, cb * D:(cb + 1) * D]
            bs = a_wcb[:, cb * D:(cb + 1) * D]
            m_ld = S.op("sync", lambda y, csrc=csrc, fs=fs: y.dma_start(fs, csrc),
                        waits=(cast_prev[cb],), inc="d")
            m_cp = S.op("gpsimd", lambda g, fs=fs, bs=bs: g.tensor_copy(bs, fs),
                        waits=(m_ld,), inc="g")
            m_st = S.op("sync", lambda y, cdst=cdst, bs=bs: y.dma_start(cdst, bs),
                        waits=(m_cp,), inc="d")
            cast_prev[cb] = m_st
            cast_marks.append(m_st)

        xn_store = []
        xq_store = [None] * NT
        m_xnt_tile = [None] * NT
        xtile_free = [(), ()]
        prev_sq_reader = None
        prev_qd = {"out": None, "sgn": None}
        for t in range(NT):
            b = t % 2
            xa = a_xf[:, b * D:(b + 1) * D]
            m_xl = S.op("sync", lambda y, t=t, xa=xa: y.dma_start(
                xa, x_in[t * 128:(t + 1) * 128, :]),
                waits=xtile_free[b], inc="d")
            c0 = a_st[:, 64 + 4 * b + 0:64 + 4 * b + 1]
            c1 = a_st[:, 64 + 4 * b + 1:64 + 4 * b + 2]
            c3 = a_st[:, 64 + 4 * b + 3:64 + 4 * b + 4]
            c2 = a_st[:, 2 * t:2 * t + 1]
            cr = a_st[:, 2 * t + 1:2 * t + 2]
            m1 = S.op("vector", lambda v, xa=xa, c0=c0: v.reduce_sum(
                c0, xa, axis=AX), waits=(m_xl,), inc="v")
            m2 = S.op("scalar", lambda a, xa=xa: a.activation(
                a_xsq[:, :], xa, AF.Square),
                waits=(m_xl, prev_sq_reader), inc="a")
            m3 = S.op("vector", lambda v, c1=c1: v.reduce_sum(
                c1, a_xsq[:, :], axis=AX), waits=(m2,), inc="v")
            m4 = S.op("vector", lambda v, c0=c0, c2=c2: v.tensor_single_scalar(
                c2, c0, 1.0 / D, OP.mult), waits=(m1,), inc="v")
            m5 = S.op("vector", lambda v, c2=c2, c3=c3: v.tensor_mul(
                c3, c2, c2), waits=(m4,), inc="v")
            m6a = S.op("vector", lambda v, c1=c1: v.tensor_single_scalar(
                c1, c1, 1.0 / D, OP.mult), waits=(m3,), inc="v")
            m6 = S.op("vector", lambda v, c0=c0, c1=c1, c3=c3: v.tensor_tensor(
                c0, c1, c3, op=OP.subtract), waits=(m6a, m5), inc="v")
            m7 = S.op("vector", lambda v, c0=c0: v.tensor_single_scalar(
                c0, c0, 1e-5, OP.add), waits=(m6,), inc="v")
            m8 = S.op("scalar", lambda a, c0=c0, c1=c1: a.activation(
                c1, c0, AF.Ln), waits=(m7,), inc="a")
            m9 = S.op("vector", lambda v, c1=c1, c3=c3: v.tensor_single_scalar(
                c3, c1, -0.5, OP.mult), waits=(m8,), inc="v")
            m10 = S.op("scalar", lambda a, c3=c3, cr=cr: a.activation(
                cr, c3, AF.Exp), waits=(m9,), inc="a")
            xnb = a_xnbf[:, b * D:(b + 1) * D]
            m11 = S.op("vector", lambda v, xa=xa, xnb=xnb, c2=c2, cr=cr:
                       v.tensor_scalar(xnb, xa, c2, cr,
                                       op0=OP.subtract, op1=OP.mult),
                       waits=(m10, xn_store[t - 2] if t >= 2 else None),
                       inc="v")
            m_xn_st = S.op("sync", lambda y, t=t, xnb=xnb: y.dma_start(
                xn_scr[t * 128:(t + 1) * 128, :], xnb), waits=(m11,), inc="d")
            xn_store.append(m_xn_st)
            prev_sq_reader = m3
            m12 = S.op("vector", lambda v, xa=xa, c2=c2, cr=cr:
                       v.tensor_scalar(a_xnf[:, :], xa, c2, cr,
                                       op0=OP.subtract, op1=OP.mult),
                       waits=(m10, prev_qd["out"], prev_qd["sgn"]), inc="v")
            xqb = a_xqbf[:, b * D:(b + 1) * D]
            qd = emit_quant(a_xnf[:, :], xqb, D, w_in=(m12,),
                            w_out=(xq_store[t - 2] if t >= 2 else None,))
            prev_qd = qd
            xq_store[t] = S.op("sync", lambda y, t=t, xqb=xqb: y.dma_start(
                xq_scr[t * 128:(t + 1) * 128, :], xqb),
                waits=(qd["out"],), inc="d")
            xtile_free[b] = (m11, m12, m10)
            emit_cast_one(); emit_cast_one(); emit_cast_one()
            for c in range(DC):
                m_xnt_tile[t] = S.op("sync", lambda y, t=t, c=c: y.dma_start(
                    a_xnt[:, c * N + t * 128:c * N + (t + 1) * 128],
                    xn_scr[t * 128:(t + 1) * 128, c * 128:(c + 1) * 128],
                    transpose=True), waits=(m_xn_st,), inc="d")

        while cast_i[0] < len(cast_jobs):
            emit_cast_one()
        m_wq_cast = cast_marks[3]
        m_wk_cast = cast_marks[7]
        m_wv_cast = cast_marks[71]
        m_wv_cast_slab = lambda h: cast_marks[8 + 8 * h + 7]

        m_dummy_removed = None

        m_xnt = m_xnt_tile[NT - 1]

        m_wqt = m_wkt = None
        for c in range(DC):
            m_wqt = S.op("sync", lambda y, c=c: y.dma_start(
                a_wqt[:, c * 512:(c + 1) * 512],
                wq_bf[:, c * 128:(c + 1) * 128], transpose=True),
                waits=(m_wq_cast,), inc="d")
            m_wkt = S.op("sync", lambda y, c=c: y.dma_start(
                a_wkt[:, c * 512:(c + 1) * 512],
                wk_bf[:, c * 128:(c + 1) * 128], transpose=True),
                waits=(m_wk_cast,), inc="d")

        qk_store = {}
        maxabs_marks = []
        qk_pp = 0
        qk_evac_free = [None, None]
        qk_eb_free = [None, None]
        for which, wslab, scr, mtw in (("q", a_wqt, qt_scr, m_wqt),
                                       ("k", a_wkt, kt_scr, m_wkt)):
            for pc in range(4):
                for g in range(G):
                    bank = pO[qk_pp % 2]
                    eb = a_qke[:, (qk_pp % 2) * 512:(qk_pp % 2 + 1) * 512]
                    mm = None
                    for c in range(DC):
                        w0 = (mtw, m_xnt_tile[4 * g + 3],
                              qk_evac_free[qk_pp % 2]) if c == 0 else ()
                        mm = S.op("tensor", lambda T, bank=bank, wslab=wslab,
                                  c=c, pc=pc, g=g: T.matmul(
                                      bank[:, :],
                                      wslab[:, c * 512 + pc * 128:
                                            c * 512 + (pc + 1) * 128],
                                      a_xnt[:, c * N + g * GW:
                                            c * N + (g + 1) * GW],
                                      start=(c == 0), stop=(c == DC - 1)),
                                  waits=w0, inc="t" if c == DC - 1 else None)
                    m_ev = S.op("vector", lambda v, bank=bank, eb=eb:
                                v.tensor_copy(eb, bank[:, :]),
                                waits=(mm, qk_eb_free[qk_pp % 2]), inc="v")
                    m_st = S.op("sync", lambda y, scr=scr, pc=pc, g=g, eb=eb:
                                y.dma_start(scr[pc * 128:(pc + 1) * 128,
                                                g * GW:(g + 1) * GW], eb),
                                waits=(m_ev,), inc="d")
                    qk_store[(which, pc, g)] = m_st
                    qk_eb_free[qk_pp % 2] = m_st
                    col = pc + (0 if which == "q" else 4)
                    if g == 0:
                        mrm = S.op("vector", lambda v, eb=eb, col=col:
                                   v.tensor_reduce(
                                       smm[:, col:col + 1], eb, axis=AX,
                                       op=OP.max, apply_absolute_value=True),
                                   inc="v")
                    else:
                        m_r2 = S.op("vector", lambda v, eb=eb: v.tensor_reduce(
                            smm[:, 15:16], eb, axis=AX, op=OP.max,
                            apply_absolute_value=True), inc="v")
                        mrm = S.op("vector", lambda v, col=col: v.tensor_max(
                            smm[:, col:col + 1], smm[:, col:col + 1],
                            smm[:, 15:16]), waits=(m_r2,), inc="v")
                    qk_evac_free[qk_pp % 2] = mrm
                    if g == G - 1:
                        maxabs_marks.append(mrm)
                    qk_pp += 1
        m_qk_evac_last = qk_evac_free[(qk_pp - 1) % 2]

        m_sq = S.op("vector", lambda v: v.tensor_mul(
            smm[:, 16:24], smm[:, 0:8], smm[:, 0:8]),
            waits=tuple(maxabs_marks), inc="v")
        m_sqb = S.op("vector", lambda v: v.tensor_copy(
            cbf[:, 0:8], smm[:, 16:24]), waits=(m_sq,), inc="v")
        m_csum = None
        for idx in range(16):
            col = idx % 8
            lo = 0 if idx < 8 else 64
            m_csum = S.op("tensor", lambda T, idx=idx, col=col, lo=lo: T.matmul(
                pDen[0:1, idx:idx + 1], cbf[lo:lo + 64, col:col + 1],
                ones[lo:lo + 64, 0:1], start=(idx == 0), stop=(idx == 15),
                tile_position=(lo, 0), skip_group_check=True),
                waits=(m_sqb, m_ones), inc="t")
        m_cs = S.op("vector", lambda v: v.tensor_copy(
            smm[0:1, 24:40], pDen[0:1, 0:16]), waits=(m_csum,), inc="v")
        m_ch = []
        for h in range(H):
            pc2, par = h // 2, h % 2
            qc = 24 + par * 8 + pc2
            kc = 24 + par * 8 + 4 + pc2
            ma1 = S.op("vector", lambda v, h=h, qc=qc, kc=kc: v.tensor_add(
                smm[0:1, 40 + h:41 + h], smm[0:1, qc:qc + 1],
                smm[0:1, kc:kc + 1]), waits=(m_cs,), inc="v")
            ma2 = S.op("vector", lambda v, h=h: v.tensor_single_scalar(
                smm[0:1, 40 + h:41 + h], smm[0:1, 40 + h:41 + h], -0.5,
                OP.mult), waits=(ma1,), inc="v")
            m_ch.append(S.op("vector", lambda v, h=h: v.tensor_copy(
                cbf[0:1, 8 + h:9 + h], smm[0:1, 40 + h:41 + h]),
                waits=(ma2,), inc="v"))
        m_cneg = []
        prev_cp = m_cs
        for h in range(H):
            mM = S.op("tensor", lambda T, h=h: T.matmul(
                pDen[:, 0:1], ones[0:1, :], cbf[0:1, 8 + h:9 + h],
                start=True, stop=True, skip_group_check=True),
                waits=(m_ch[h], prev_cp), inc="t")
            mC = S.op("scalar", lambda a, h=h: a.copy(
                cneg[:, h:h + 1], pDen[:, 0:1]), waits=(mM,), inc="a")
            prev_cp = mC
            m_cneg.append(mC)

        v_spilled = {}
        wvt_free = [None, None]
        vout_store = None
        v_evac_last = None
        v_evac_waits = {}
        for h in range(H):
            wb = h % 2
            m_wvt = None
            for c in range(DC):
                m_wvt = S.op("sync", lambda y, h=h, c=c, wb=wb: y.dma_start(
                    a_wvt[:, wb * DC * D + c * D:wb * DC * D + (c + 1) * D],
                    wv_bf[h * D:(h + 1) * D, c * 128:(c + 1) * 128],
                    transpose=True),
                    waits=(m_wv_cast_slab(h), wvt_free[wb]), inc="d")
            mm = None
            for mc in range(NT):
                k8 = mc % 8
                pa = pS[2 * (mc % 2)]
                pb2 = pS[2 * (mc % 2) + 1]
                for c in range(DC):
                    w0 = ()
                    if c == 0:
                        w0 = (m_wvt, m_xnt_tile[mc], v_evac_waits.get(mc % 2))
                    S.op("tensor", lambda T, pa=pa, c=c, mc=mc, wb=wb: T.matmul(
                        pa[:, :],
                        a_xnt[:, c * N + mc * 128:c * N + (mc + 1) * 128],
                        a_wvt[:, wb * DC * D + c * D:wb * DC * D + c * D + 512],
                        start=(c == 0), stop=(c == DC - 1)), waits=w0)
                    mm = S.op("tensor", lambda T, pb2=pb2, c=c, mc=mc, wb=wb:
                              T.matmul(
                                  pb2[:, :],
                                  a_xnt[:, c * N + mc * 128:c * N + (mc + 1) * 128],
                                  a_wvt[:, wb * DC * D + c * D + 512:
                                        wb * DC * D + (c + 1) * D],
                                  start=(c == 0), stop=(c == DC - 1)),
                              inc="t" if c == DC - 1 else None, no_ldw=True)
                d0 = a_vout[:, 0 * 4096 + k8 * 512:0 * 4096 + (k8 + 1) * 512]
                d1 = a_vout[:, 1 * 4096 + k8 * 512:1 * 4096 + (k8 + 1) * 512]
                S.op("scalar", lambda a, pa=pa, d0=d0: a.copy(
                    d0, pa[:, :]),
                    waits=(mm, vout_store if k8 == 0 else None), inc="a")
                m_ev = S.op("scalar", lambda a, pb2=pb2, d1=d1: a.copy(
                    d1, pb2[:, :]), inc="a")
                v_evac_waits[mc % 2] = m_ev
                v_evac_last = m_ev
                if k8 == 7:
                    blk8 = mc // 8
                    for half in range(2):
                        vout_store = S.op(
                            "sync", lambda y, h=h, half=half, blk8=blk8:
                            y.dma_start(
                                v_scr[h, half, :, blk8 * 8:(blk8 + 1) * 8, :],
                                a_vout[:, half * 4096:(half + 1) * 4096]
                                .rearrange("p (k d) -> p k d", d=512)),
                            waits=(m_ev,), inc="d")
                    v_spilled[(h, blk8)] = vout_store
            wvt_free[wb] = mm

        if debug:
            dbg_c = nc.dram_tensor("dbg_c", [128, 64 + 8], F32,
                                   kind="ExternalOutput").ap()
            S.op("sync", lambda y: y.dma_start(dbg_c[:, 0:8], cneg[:, :]),
                 waits=(m_cneg[-1],), inc="d")
            S.op("sync", lambda y: y.dma_start(dbg_c[:, 8:72], smm[:, :]),
                 inc="d")
            dbg_a = nc.dram_tensor("dbg_a", [128, 80], F32,
                                   kind="ExternalOutput").ap()
            dbg_xnf = nc.dram_tensor("dbg_xnf", [128, D], F32,
                                     kind="ExternalOutput").ap()
            dbg_xqb = nc.dram_tensor("dbg_xqb", [128, 2 * D], BF16,
                                     kind="ExternalOutput").ap()
            S.op("sync", lambda y: y.dma_start(dbg_a, a_st[:, :]),
                 waits=(prev_qd["out"], prev_qd["sgn"]), inc="d")
            S.op("sync", lambda y: y.dma_start(dbg_xnf, a_xnf[:, :]), inc="d")
            S.op("sync", lambda y: y.dma_start(dbg_xqb, a_xqbf[:, :]), inc="d")
            dbg_qp = nc.dram_tensor("dbg_qp", [128, 5 * 1024 + 3 * 176], F32,
                                    kind="ExternalOutput").ap()
            for qi in range(5):
                S.op("sync", lambda y, qi=qi: y.dma_start(
                    dbg_qp[:, qi * 1024:(qi + 1) * 1024], qp[qi][:, :]), inc="d")
            S.op("sync", lambda y: y.dma_start(
                dbg_qp[:, 5120:5296], thrp[:, :]), inc="d")
            S.op("sync", lambda y: y.dma_start(
                dbg_qp[:, 5296:5472], thr6[:, :]), inc="d")
            S.op("sync", lambda y: y.dma_start(
                dbg_qp[:, 5472:5648], tks[0][:, :]), inc="d")

        S.fence()
        stA.close()
        stB = ExitStack()
        e = stB.enter_context
        b_kt = e(nc.sbuf_tensor("b_kt", [128, 2 * N], BF16))
        b_qt = e(nc.sbuf_tensor("b_qt", [128, 2 * 512], BF16))
        b_et = e(nc.sbuf_tensor("b_et", [128, 2 * NT * 512], BF16))
        b_vb = e(nc.sbuf_tensor("b_vb", [128, 2 * NT * 512], BF16))
        b_oacc = e(nc.sbuf_tensor("b_oacc", [128, 4 * D], F32))
        b_rden = e(nc.sbuf_tensor("b_rden", [128, 8], F32))
        b_wpf = e(nc.sbuf_tensor("b_wpf", [128, D], F32))
        b_wpb = e(nc.sbuf_tensor("b_wpb", [128, D], BF16))

        wprep_jobs = []
        for s in range(22):
            wprep_jobs.append((wg_in[s * 128:(s + 1) * 128, :],
                               wgq_scr[s * 128:(s + 1) * 128, :]))
        for s in range(22):
            wprep_jobs.append((wu_in[s * 128:(s + 1) * 128, :],
                               wuq_scr[s * 128:(s + 1) * 128, :]))
        for s in range(8):
            for sec in range(3):
                w = 1024 if sec < 2 else 768
                wprep_jobs.append((wd_in[s * 128:(s + 1) * 128,
                                         sec * 1024:sec * 1024 + w],
                                   wdq_scr[s * 128:(s + 1) * 128,
                                           sec * 1024:sec * 1024 + w]))
        wprep_i = [0]
        wprep_st = [None]
        wprep_qd = [{"out": None, "sgn": None}]

        def emit_wprep_one():
            if wprep_i[0] >= len(wprep_jobs):
                return
            src, dst = wprep_jobs[wprep_i[0]]
            W = src.shape[1]
            wprep_i[0] += 1
            m_ld = S.op("sync", lambda y, src=src, W=W: y.dma_start(
                b_wpf[:, 0:W], src),
                waits=(wprep_qd[0]["out"], wprep_qd[0]["sgn"]), inc="d")
            qd = emit_quant(b_wpf[:, 0:W], b_wpb[:, 0:W], W,
                            w_in=(m_ld,), w_out=(wprep_st[0],))
            wprep_qd[0] = qd
            wprep_st[0] = S.op("sync", lambda y, dst=dst, W=W: y.dma_start(
                dst, b_wpb[:, 0:W]), waits=(qd["out"],), inc="d")

        kt_free = [m_qk_evac_last, m_qk_evac_last]
        vb_free = [v_evac_last, v_evac_last]
        pO_free = {0: m_qk_evac_last, 1: m_qk_evac_last, 2: None}
        pden_free = m_cneg[-1]
        et_consumed = [v_evac_last, v_evac_last]
        oacc_stored = None
        o_rc = [0]
        exp_marks = None

        for g in range(G):
            for p in range(4):
                pb = p % 2
                m_kt = S.op("sync", lambda y, p=p, pb=pb: y.dma_start(
                    b_kt[:, pb * N:(pb + 1) * N],
                    kt_scr[p * 128:(p + 1) * 128, :]),
                    waits=(qk_store[("k", p, G - 1)], kt_free[pb]), inc="d")
                m_qt = S.op("sync", lambda y, p=p, g=g, pb=pb: y.dma_start(
                    b_qt[:, pb * 512:(pb + 1) * 512],
                    qt_scr[p * 128:(p + 1) * 128, g * GW:(g + 1) * GW]),
                    waits=(qk_store[("q", p, g)],), inc="d")
                exp_marks = {0: {}, 1: {}}
                m_s = None
                for mc in range(NT):
                    pA = pS[mc % 2]
                    pB = pS[2 + mc % 2]
                    wA = [m_kt, m_qt]
                    if mc >= 2:
                        wA += [exp_marks[0][mc - 2], exp_marks[1][mc - 2]]
                    else:
                        wA += [et_consumed[0], et_consumed[1]]
                    S.op("tensor", lambda T, pA=pA, pb=pb, mc=mc: T.matmul(
                        pA[:, :],
                        b_kt[0:64, pb * N + mc * 128:pb * N + (mc + 1) * 128],
                        b_qt[0:64, pb * 512:(pb + 1) * 512],
                        start=True, stop=True), waits=tuple(wA))
                    m_s = S.op("tensor", lambda T, pB=pB, pb=pb, mc=mc: T.matmul(
                        pB[:, :],
                        b_kt[64:128, pb * N + mc * 128:pb * N + (mc + 1) * 128],
                        b_qt[64:128, pb * 512:(pb + 1) * 512],
                        start=True, stop=True, tile_position=(64, 0)), inc="t")
                    for hh in range(2):
                        h = 2 * p + hh
                        psrc = pA if hh == 0 else pB
                        exp_marks[hh][mc] = S.op(
                            "scalar", lambda a, hh=hh, mc=mc, psrc=psrc, h=h:
                            a.activation(
                                b_et[:, hh * NT * 512 + mc * 512:
                                     hh * NT * 512 + (mc + 1) * 512],
                                psrc[:, :], AF.Exp, bias=cneg[:, h:h + 1]),
                            waits=(m_s, m_cneg[h]), inc="a")
                kt_free[pb] = m_s
                for hh in range(2):
                    h = 2 * p + hh
                    m_exp_last = exp_marks[hh][NT - 1]
                    mm_last = None
                    m_vl = {}
                    m_rc_sp = {}
                    for sp in range(2):
                        for half in range(2):
                            vbuf = half
                            if sp == 0:
                                m_vl[half] = S.op(
                                    "sync", lambda y, h=h, half=half, vbuf=vbuf:
                                    y.dma_start(
                                        b_vb[:, vbuf * NT * 512:
                                             (vbuf + 1) * NT * 512],
                                        v_scr[h, half, :, :, :].rearrange(
                                            "p k d -> p (k d)")),
                                    waits=(v_spilled[(h, 3)], vb_free[vbuf]),
                                    inc="d")
                            rc = o_rc[0]
                            o_rc[0] += 1
                            banks = (pO[(2 * rc) % 3], pO[(2 * rc + 1) % 3])
                            for mc in range(NT):
                                for si, sub in enumerate((2 * sp, 2 * sp + 1)):
                                    bk = banks[si]
                                    w0 = ()
                                    if mc == 0:
                                        w0 = (m_exp_last, m_vl[half],
                                              pO_free[(2 * rc + si) % 3])
                                        if half == 0 and si == 0:
                                            w0 = w0 + ((pden_free,) if sp == 0
                                                       else (m_rc_sp[0],))
                                    is_last = (mc == NT - 1 and si == 1)
                                    mm_last = S.op(
                                        "tensor",
                                        lambda T, bk=bk, hh=hh, mc=mc, sub=sub,
                                        vbuf=vbuf: T.matmul(
                                            bk[:, :],
                                            b_et[:, hh * NT * 512 + mc * 512
                                                 + sub * 128:
                                                 hh * NT * 512 + mc * 512
                                                 + (sub + 1) * 128],
                                            b_vb[:, vbuf * NT * 512 + mc * 512:
                                                 vbuf * NT * 512
                                                 + (mc + 1) * 512],
                                            start=(mc == 0),
                                            stop=(mc == NT - 1)),
                                        waits=w0,
                                        inc="t" if (is_last and half == 1)
                                        else None)
                                    if half == 0:
                                        m_rid = S.op(
                                            "tensor", lambda T, hh=hh, mc=mc,
                                            sub=sub, sp=sp: T.matmul(
                                                pDen[:, sub:sub + 1],
                                                b_et[:, hh * NT * 512
                                                     + mc * 512 + sub * 128:
                                                     hh * NT * 512 + mc * 512
                                                     + (sub + 1) * 128],
                                                ones[:, 0:1],
                                                start=(mc == 0 and sub == 0
                                                       and sp == 0),
                                                stop=(mc == NT - 1 and sub == 3),
                                                skip_group_check=True),
                                            inc="t" if is_last else None,
                                            no_ldw=True)
                                        if is_last:
                                            mm_last = m_rid
                            if half == 0:
                                m_rc_sp[sp] = S.op(
                                    "vector", lambda v, sp=sp: v.reciprocal(
                                        b_rden[:, 2 * sp:2 * sp + 2],
                                        pDen[:, 2 * sp:2 * sp + 2]),
                                    waits=(mm_last,), inc="v")
                                if sp == 1:
                                    pden_free = m_rc_sp[1]
                            for si, sub in enumerate((2 * sp, 2 * sp + 1)):
                                bk = banks[si]
                                dst = b_oacc[:, sub * D + half * 512:
                                             sub * D + half * 512 + 512]
                                w0 = (mm_last, m_rc_sp[sp], oacc_stored)
                                if h == 0:
                                    m_ev = S.op(
                                        "vector", lambda v, bk=bk, dst=dst,
                                        sub=sub: v.tensor_scalar(
                                            dst, bk[:, :],
                                            b_rden[:, sub:sub + 1], None,
                                            op0=OP.mult),
                                        waits=w0, inc="v")
                                else:
                                    m_ev = S.op(
                                        "vector", lambda v, bk=bk, dst=dst,
                                        sub=sub: v.scalar_tensor_tensor(
                                            dst, bk[:, :],
                                            b_rden[:, sub:sub + 1], dst,
                                            op0=OP.mult, op1=OP.add),
                                        waits=w0, inc="v")
                                pO_free[(2 * rc + si) % 3] = m_ev
                            if sp == 1:
                                vb_free[vbuf] = mm_last
                    et_consumed[hh] = mm_last
                emit_wprep_one()
                emit_wprep_one()
            last_ev = pO_free[(2 * (o_rc[0] - 1) + 1) % 3]
            for sub in range(4):
                oacc_stored = S.op("sync", lambda y, g=g, sub=sub: y.dma_start(
                    oacc_scr[g * GW + sub * 128:g * GW + (sub + 1) * 128, :],
                    b_oacc[:, sub * D:(sub + 1) * D]),
                    waits=(last_ev,), inc="d")
        while wprep_i[0] < len(wprep_jobs):
            emit_wprep_one()
        m_wprep_done = wprep_st[0]
        b_last_exp = [exp_marks[0][NT - 1], exp_marks[1][NT - 1]]

        S.fence()
        stB.close()
        stC = ExitStack()
        e = stC.enter_context
        c_wg = e(nc.sbuf_tensor("c_wg", [128, DC * I], BF16))
        c_wu = e(nc.sbuf_tensor("c_wu", [128, DC * I], BF16))
        c_wdt = e(nc.sbuf_tensor("c_wdt", [128, IC * D], BF16))
        c_xqt = e(nc.sbuf_tensor("c_xqt", [128, 2 * DC * 128], BF16))
        c_silu = e(nc.sbuf_tensor("c_silu", [128, 512], F32))
        c_hf = e(nc.sbuf_tensor("c_hf", [128, 512], F32))
        c_hq = e(nc.sbuf_tensor("c_hq", [128, 2 * I], BF16))
        c_hqt = e(nc.sbuf_tensor("c_hqt", [128, 2 * IC * 128], BF16))
        c_xo = e(nc.sbuf_tensor("c_xo", [128, 2 * D], F32))
        c_out = e(nc.sbuf_tensor("c_out", [128, D], F32))

        m_cw = None
        for c in range(DC):
            m_cw = S.op("sync", lambda y, c=c: y.dma_start(
                c_wg[:, c * I:(c + 1) * I],
                wgq_scr[:, c * 128:(c + 1) * 128], transpose=True),
                waits=(m_wprep_done,), inc="d")
            m_cw = S.op("sync", lambda y, c=c: y.dma_start(
                c_wu[:, c * I:(c + 1) * I],
                wuq_scr[:, c * 128:(c + 1) * 128], transpose=True), inc="d")
        for ic in range(IC):
            m_cw = S.op("sync", lambda y, ic=ic: y.dma_start(
                c_wdt[:, ic * D:(ic + 1) * D],
                wdq_scr[:, ic * 128:(ic + 1) * 128], transpose=True), inc="d")

        SEGS = [(0, 512), (512, 1024), (1024, 1536), (1536, 2048),
                (2048, 2560), (2560, 2816)]
        xqt_free = [b_last_exp[0], b_last_exp[1]]
        xo_free = None
        hq_seg_store = [[None] * len([1]*6) for _ in range(2)]
        hqt_free = [None, None]
        out_stored = None
        seg_ps_free = {0: b_last_exp[0], 1: b_last_exp[0],
                       2: b_last_exp[1], 3: b_last_exp[1]}
        down_free = {0: None, 1: None}
        prev_cq = [{"out": None, "sgn": None}]
        for nb in range(NT):
            b = nb % 2
            m_xqt = None
            for c in range(DC):
                m_xqt = S.op("sync", lambda y, nb=nb, c=c, b=b: y.dma_start(
                    c_xqt[:, b * DC * 128 + c * 128:b * DC * 128 + (c + 1) * 128],
                    xq_scr[nb * 128:(nb + 1) * 128, c * 128:(c + 1) * 128],
                    transpose=True),
                    waits=(xq_store[nb], xqt_free[b]), inc="d")
            m_x = S.op("sync", lambda y, nb=nb: y.dma_start(
                c_xo[:, 0:D], x_in[nb * 128:(nb + 1) * 128, :]),
                waits=(xo_free,), inc="d")
            m_o = S.op("sync", lambda y, nb=nb: y.dma_start(
                c_xo[:, D:2 * D], oacc_scr[nb * 128:(nb + 1) * 128, :]),
                waits=(oacc_stored,), inc="d")
            m_seg_out = []
            mm_u = None
            m_hqt_ic = [None] * IC
            for si, (s0, s1) in enumerate(SEGS):
                W = s1 - s0
                pg = pS[2 * (si % 2)]
                pu = pS[2 * (si % 2) + 1]
                for c in range(DC):
                    w0 = ()
                    if c == 0:
                        w0 = (m_xqt, m_cw, seg_ps_free[2 * (si % 2)],
                              seg_ps_free[2 * (si % 2) + 1])
                    S.op("tensor", lambda T, pg=pg, c=c, s0=s0, s1=s1, b=b:
                         T.matmul(pg[:, 0:s1 - s0],
                                  c_xqt[:, b * DC * 128 + c * 128:
                                        b * DC * 128 + (c + 1) * 128],
                                  c_wg[:, c * I + s0:c * I + s1],
                                  start=(c == 0), stop=(c == DC - 1)),
                         waits=w0)
                    mm_u = S.op("tensor", lambda T, pu=pu, c=c, s0=s0, s1=s1,
                                b=b: T.matmul(
                                    pu[:, 0:s1 - s0],
                                    c_xqt[:, b * DC * 128 + c * 128:
                                          b * DC * 128 + (c + 1) * 128],
                                    c_wu[:, c * I + s0:c * I + s1],
                                    start=(c == 0), stop=(c == DC - 1)),
                                inc="t" if c == DC - 1 else None, no_ldw=True)
                m_si = S.op("scalar", lambda a, pg=pg, W=W: a.activation(
                    c_silu[:, 0:W], pg[:, 0:W], AF.Silu),
                    waits=(mm_u, prev_cq[0]["out"]), inc="a")
                m_h = S.op("vector", lambda v, pu=pu, W=W: v.tensor_mul(
                    c_hf[:, 0:W], c_silu[:, 0:W], pu[:, 0:W]),
                    waits=(m_si, prev_cq[0]["out"]), inc="v")
                seg_ps_free[2 * (si % 2)] = m_h
                seg_ps_free[2 * (si % 2) + 1] = m_h
                qd = emit_quant(c_hf[:, 0:W], c_hq[:, b * I + s0:b * I + s1],
                                W, w_in=(m_h,),
                                w_out=(hq_seg_store[b][si],))
                prev_cq[0] = qd
                m_seg_out.append(qd["out"])
                m_sst = S.op("sync", lambda y, nb=nb, s0=s0, s1=s1, b=b:
                             y.dma_start(
                                 hq_scr[nb * 128:(nb + 1) * 128, s0:s1],
                                 c_hq[:, b * I + s0:b * I + s1]),
                             waits=(qd["out"],), inc="d")
                hq_seg_store[b][si] = m_sst
                ic0, ic1 = s0 // 128, s1 // 128
                for ic in range(ic0, ic1):
                    m_hqt_ic[ic] = S.op(
                        "sync", lambda y, nb=nb, ic=ic, b=b: y.dma_start(
                            c_hqt[:, b * IC * 128 + ic * 128:
                                  b * IC * 128 + (ic + 1) * 128],
                            hq_scr[nb * 128:(nb + 1) * 128,
                                   ic * 128:(ic + 1) * 128],
                            transpose=True),
                        waits=(m_sst, hqt_free[b]), inc="d")
            mm_d = None
            for ic in range(IC):
                w0 = (m_hqt_ic[ic],)
                if ic == 0:
                    w0 = w0 + (down_free[0], down_free[1])
                S.op("tensor", lambda T, ic=ic, b=b: T.matmul(
                    pO[0][:, :],
                    c_hqt[:, b * IC * 128 + ic * 128:b * IC * 128 + (ic + 1) * 128],
                    c_wdt[:, ic * D:ic * D + 512],
                    start=(ic == 0), stop=(ic == IC - 1)), waits=w0)
                mm_d = S.op("tensor", lambda T, ic=ic, b=b: T.matmul(
                    pO[1][:, :],
                    c_hqt[:, b * IC * 128 + ic * 128:b * IC * 128 + (ic + 1) * 128],
                    c_wdt[:, ic * D + 512:(ic + 1) * D],
                    start=(ic == 0), stop=(ic == IC - 1)),
                    inc="t" if ic == IC - 1 else None, no_ldw=True)
            hqt_free[b] = mm_d
            m_xo2 = S.op("vector", lambda v: v.tensor_add(
                c_xo[:, 0:D], c_xo[:, 0:D], c_xo[:, D:2 * D]),
                waits=(m_x, m_o), inc="v")
            mo0 = S.op("vector", lambda v: v.tensor_add(
                c_out[:, 0:512], pO[0][:, :], c_xo[:, 0:512]),
                waits=(mm_d, m_xo2, out_stored), inc="v")
            mo1 = S.op("vector", lambda v: v.tensor_add(
                c_out[:, 512:D], pO[1][:, :], c_xo[:, 512:D]), inc="v")
            down_free[0] = mo0
            down_free[1] = mo1
            xo_free = mo1
            xqt_free[b] = mm_u
            out_stored = S.op("sync", lambda y, nb=nb: y.dma_start(
                out_t[nb * 128:(nb + 1) * 128, :], c_out[:, :]),
                waits=(mo1,), inc="d")

        S.op("sync", lambda y: None, waits=(("d", S.cnt["d"]),))
        stC.close()

        _EMIT_GUARD = True
        with (nc.semaphore("sem_d") as hd,
              nc.semaphore("sem_t") as ht,
              nc.semaphore("sem_v") as hv,
              nc.semaphore("sem_a") as ha,
              nc.semaphore("sem_e") as he,
              nc.semaphore("sem_g") as hg,
              nc.Block() as blk):
            handles = {"d": hd, "t": ht, "v": hv, "a": ha, "e": he, "g": hg}

            @blk.sync
            def _(eng):
                S.emit("sync", eng, handles)

            @blk.tensor
            def _(eng):
                S.emit("tensor", eng, handles)

            @blk.vector
            def _(eng):
                S.emit("vector", eng, handles)

            @blk.scalar
            def _(eng):
                S.emit("scalar", eng, handles)

            @blk.gpsimd
            def _(eng):
                S.emit("gpsimd", eng, handles)

    import dataclasses
    names = getattr(S, "no_ldw_names", set())
    if names:
        for f in nc.m.functions:
            for bb in f.blocks:
                insts = bb.instructions
                for i, ins in enumerate(insts):
                    if isinstance(ins, mybir.InstMatmult) and ins.name in names:
                        insts[i] = dataclasses.replace(ins, ldweights=False)
    return nc


_CACHE = {}


def kernel(L0_attn_x, L0_attn_Wq, L0_attn_Wk, L0_attn_Wv,
           L0_ffn_w_gate, L0_ffn_w_up, L0_ffn_w_down):
    if "nc" not in _CACHE:
        _CACHE["nc"] = build_program()
    nc = _CACHE["nc"]
    x = np.ascontiguousarray(np.asarray(L0_attn_x, dtype=np.float32))
    in_common = dict(
        wq=np.ascontiguousarray(np.asarray(L0_attn_Wq, dtype=np.float32)),
        wk=np.ascontiguousarray(np.asarray(L0_attn_Wk, dtype=np.float32)),
        wv=np.ascontiguousarray(np.asarray(L0_attn_Wv, dtype=np.float32)),
        wg=np.ascontiguousarray(np.asarray(L0_ffn_w_gate, dtype=np.float32)),
        wu=np.ascontiguousarray(np.asarray(L0_ffn_w_up, dtype=np.float32)),
        wd=np.ascontiguousarray(np.asarray(L0_ffn_w_down, dtype=np.float32)))
    in_maps = [dict(x=np.ascontiguousarray(x[i]), **in_common)
               for i in range(8)]
    res = run_bass_kernel_spmd(nc, in_maps, core_ids=list(range(8)))
    return np.stack([np.asarray(r["out"], dtype=np.float32)
                     for r in res.results], axis=0)


# version 11
# speedup vs baseline: 1.0665x; 1.0482x over previous
import numpy as np
import concourse.bass as bass
import concourse.mybir as mybir
from concourse.bass_utils import run_bass_kernel_spmd

F32 = mybir.dt.float32
BF16 = mybir.dt.bfloat16
AF = mybir.ActivationFunctionType
OP = mybir.AluOpType
AX = mybir.AxisListType.X

N = 4096
D = 1024
H = 8
I = 2816
NT = N // 128
DC = D // 128
G = 8
GW = 512
IC = I // 128

MIDS = [0.25, 0.75, 1.25, 1.75, 2.5, 3.5, 5.0]

ENGS = ("sync", "tensor", "vector", "scalar")


class Sched:

    def __init__(self):
        self.ops = {e: [] for e in ENGS}
        self.cnt = {s: 0 for s in "dtva"}
        self.pending_fence = {}

    def op(self, eng, fn, waits=(), inc=None, no_ldw=False):
        waits = [w for w in waits if w is not None]
        if eng in self.pending_fence:
            waits = list(self.pending_fence.pop(eng)) + waits
        mark = None
        if inc is not None:
            self.cnt[inc] += 16 if inc == "d" else 1
            mark = (inc, self.cnt[inc])
        self.ops[eng].append((fn, tuple(waits), inc, mark, no_ldw))
        return mark

    def fence(self):
        snap = tuple((s, v) for s, v in self.cnt.items() if v > 0)
        for e in ENGS:
            self.pending_fence[e] = snap

    def emit(self, eng_name, eng, handles):
        guaranteed = {}
        self.no_ldw_names = getattr(self, "no_ldw_names", set())
        for fn, waits, inc, mark, no_ldw in self.ops[eng_name]:
            for (sem, val) in waits:
                if guaranteed.get(sem, 0) < val:
                    eng.wait_ge(handles[sem], val)
                    guaranteed[sem] = val
            inst = fn(eng)
            if inc is not None:
                inst.then_inc(handles[inc], 16 if inc == "d" else 1)
            if no_ldw:
                self.no_ldw_names.add(inst.ins.name)


def build_program(debug=False):
    nc = bass.Bass("TRN2", target_bir_lowering=False)
    IK = "ExternalOutput" if debug else "Internal"

    x_in = nc.dram_tensor("x", [N, D], F32, kind="ExternalInput").ap()
    wq_in = nc.dram_tensor("wq", [512, D], F32, kind="ExternalInput").ap()
    wk_in = nc.dram_tensor("wk", [512, D], F32, kind="ExternalInput").ap()
    wv_in = nc.dram_tensor("wv", [H * D, D], F32, kind="ExternalInput").ap()
    wg_in = nc.dram_tensor("wg", [I, D], F32, kind="ExternalInput").ap()
    wu_in = nc.dram_tensor("wu", [I, D], F32, kind="ExternalInput").ap()
    wd_in = nc.dram_tensor("wd", [D, I], F32, kind="ExternalInput").ap()
    out_t = nc.dram_tensor("out", [N, D], F32, kind="ExternalOutput").ap()

    xn_scr = nc.dram_tensor("xn_scr", [N, D], BF16, kind=IK).ap()
    xq_scr = nc.dram_tensor("xq_scr", [N, D], BF16, kind=IK).ap()
    qt_scr = nc.dram_tensor("qt_scr", [512, N], BF16, kind=IK).ap()
    kt_scr = nc.dram_tensor("kt_scr", [512, N], BF16, kind=IK).ap()
    wq_bf = nc.dram_tensor("wq_bf", [512, D], BF16, kind=IK).ap()
    wk_bf = nc.dram_tensor("wk_bf", [512, D], BF16, kind=IK).ap()
    wv_bf = nc.dram_tensor("wv_bf", [H * D, D], BF16, kind=IK).ap()
    v_scr = nc.dram_tensor("v_scr", [H, 2, 128, NT, 512], BF16, kind=IK).ap()
    oacc_scr = nc.dram_tensor("oacc_scr", [N, D], F32, kind=IK).ap()
    wgq_scr = nc.dram_tensor("wgq_scr", [I, D], BF16, kind=IK).ap()
    wuq_scr = nc.dram_tensor("wuq_scr", [I, D], BF16, kind=IK).ap()
    wdq_scr = nc.dram_tensor("wdq_scr", [D, I], BF16, kind=IK).ap()
    hq_scr = nc.dram_tensor("hq_scr", [N, I], BF16, kind=IK).ap()

    S = Sched()
    from contextlib import ExitStack
    with ExitStack() as st:
        e = st.enter_context
        ones = e(nc.sbuf_tensor("ones", [128, 128], BF16))
        cneg = e(nc.sbuf_tensor("cneg", [128, 8], F32))
        qp = [e(nc.sbuf_tensor(f"qp{i}", [128, 1024], F32)) for i in range(3)]
        qb = [e(nc.sbuf_tensor(f"qb{i}", [128, 1024], BF16)) for i in range(3)]
        thrp = e(nc.sbuf_tensor("thrp", [128, 176], F32))
        thr6 = e(nc.sbuf_tensor("thr6", [128, 176], F32))
        tks = [e(nc.sbuf_tensor(f"tk{i}", [128, 176], F32)) for i in range(7)]
        smm = e(nc.sbuf_tensor("smm", [128, 64], F32))
        cbf = e(nc.sbuf_tensor("cbf", [128, 16], BF16))

        pS = [e(nc.psum_tensor(f"pS{i}", [128, 512], F32)) for i in range(4)]
        pO = [e(nc.psum_tensor(f"pO{i}", [128, 512], F32)) for i in range(3)]
        pDen = e(nc.psum_tensor("pDen", [128, 16], F32))

        def emit_quant(in_ap, out_ap, W, w_in=(), w_out=()):
            B = W // 16
            v3 = lambda t: t[:, 0:W].rearrange("p (b i) -> p b i", i=16)
            b3 = lambda t: t[:, 0:B].rearrange("p (b o) -> p b o", o=1)
            bb = lambda t: b3(t).broadcast_to((128, B, 16))
            in3 = in_ap.rearrange("p (b i) -> p b i", i=16)
            m_thr = S.op("vector", lambda v: v.tensor_reduce(
                b3(thrp), in3, axis=AX, op=OP.max, apply_absolute_value=True),
                waits=tuple(w_in), inc="v")
            m_t6 = S.op("vector", lambda v: v.tensor_single_scalar(
                thr6[:, 0:B], thrp[:, 0:B], float(np.float32(6e-12)), OP.max),
                waits=(m_thr,), inc="v")
            m_tk0 = None
            for k in range(7):
                m_tk0 = S.op("vector", lambda v, k=k: v.tensor_single_scalar(
                    tks[k][:, 0:B], thr6[:, 0:B], MIDS[k], OP.mult),
                    waits=(m_t6,), inc="v")
            m_abs = S.op("scalar", lambda a: a.activation(
                qp[0][:, 0:W], in_ap, AF.Abs),
                waits=tuple(w_in) + (m_thr,), inc="a")
            m_sgn = S.op("scalar", lambda a: a.sign(
                qp[1][:, 0:W], in_ap), inc="a")
            S.op("vector", lambda v: v.tensor_single_scalar(
                qp[0][:, 0:W], qp[0][:, 0:W], 6.0, OP.mult),
                waits=(m_abs,), inc="v")

            def mask(k, dst):
                S.op("vector", lambda v, k=k, dst=dst: v.tensor_tensor(
                    v3(dst), v3(qp[0]), bb(tks[k]), op=OP.is_gt),
                    waits=(m_tk0,), inc="v")
            mask(0, qb[0]); mask(1, qb[1])
            S.op("vector", lambda v: v.tensor_add(
                qb[0][:, 0:W], qb[0][:, 0:W], qb[1][:, 0:W]), inc="v")
            mask(2, qb[1]); mask(3, qb[2])
            S.op("vector", lambda v: v.tensor_add(
                qb[1][:, 0:W], qb[1][:, 0:W], qb[2][:, 0:W]), inc="v")
            S.op("vector", lambda v: v.tensor_add(
                qb[0][:, 0:W], qb[0][:, 0:W], qb[1][:, 0:W]), inc="v")
            mask(4, qb[1]); mask(5, qb[2])
            S.op("vector", lambda v: v.tensor_add(
                qb[1][:, 0:W], qb[1][:, 0:W], qb[2][:, 0:W]), inc="v")
            S.op("vector", lambda v: v.scalar_tensor_tensor(
                qb[0][:, 0:W], qb[0][:, 0:W], 0.5, qb[1][:, 0:W],
                op0=OP.mult, op1=OP.add), inc="v")
            mask(6, qb[1])
            S.op("vector", lambda v: v.scalar_tensor_tensor(
                qp[2][:, 0:W], qb[1][:, 0:W], 2.0, qb[0][:, 0:W],
                op0=OP.mult, op1=OP.add), inc="v")
            S.op("vector", lambda v: v.scalar_tensor_tensor(
                v3(qp[1]), v3(qp[1]), float(np.float32(1.0 / 6.0)), bb(thr6),
                op0=OP.mult, op1=OP.mult), waits=(m_sgn,), inc="v")
            m_out = S.op("vector", lambda v: v.tensor_mul(
                out_ap, qp[2][:, 0:W], qp[1][:, 0:W]),
                waits=tuple(w_out), inc="v")
            return {"out": m_out, "sgn": m_sgn}

        m_ones = S.op("vector", lambda v: v.memset(ones[:, :], 1.0), inc="v")

        stA = ExitStack()
        e = stA.enter_context
        a_xf = e(nc.sbuf_tensor("a_xf", [128, 2 * D], F32))
        a_xsq = e(nc.sbuf_tensor("a_xsq", [128, D], F32))
        a_st = e(nc.sbuf_tensor("a_st", [128, 80], F32))
        a_xnbf = e(nc.sbuf_tensor("a_xnbf", [128, 2 * D], BF16))
        a_xnf = e(nc.sbuf_tensor("a_xnf", [128, D], F32))
        a_xqbf = e(nc.sbuf_tensor("a_xqbf", [128, 2 * D], BF16))
        a_xnt = e(nc.sbuf_tensor("a_xnt", [128, DC * N], BF16))
        a_wvt = e(nc.sbuf_tensor("a_wvt", [128, 2 * DC * D], BF16))
        a_vout = e(nc.sbuf_tensor("a_vout", [128, 2 * 8 * 512], BF16))
        a_wqt = e(nc.sbuf_tensor("a_wqt", [128, DC * 512], BF16))
        a_wkt = e(nc.sbuf_tensor("a_wkt", [128, DC * 512], BF16))
        a_qke = e(nc.sbuf_tensor("a_qke", [128, 2 * 512], BF16))
        a_wcf = e(nc.sbuf_tensor("a_wcf", [128, 2 * D], F32))
        a_wcb = e(nc.sbuf_tensor("a_wcb", [128, 2 * D], BF16))

        cast_jobs = [(wv_in[s * 128:(s + 1) * 128, :],
                      wv_bf[s * 128:(s + 1) * 128, :]) for s in range(64)]
        cast_jobs += [(wq_in[s * 128:(s + 1) * 128, :],
                       wq_bf[s * 128:(s + 1) * 128, :]) for s in range(4)]
        cast_jobs += [(wk_in[s * 128:(s + 1) * 128, :],
                       wk_bf[s * 128:(s + 1) * 128, :]) for s in range(4)]
        cast_prev = [None, None]
        cast_marks = []
        for j, (src, dst) in enumerate(cast_jobs):
            b = j % 2
            fs = a_wcf[:, b * D:(b + 1) * D]
            bs = a_wcb[:, b * D:(b + 1) * D]
            m_ld = S.op("sync", lambda y, src=src, fs=fs: y.dma_start(fs, src),
                        waits=(cast_prev[b],), inc="d")
            m_cp = S.op("scalar", lambda a, fs=fs, bs=bs: a.copy(bs, fs),
                        waits=(m_ld,), inc="a")
            m_st = S.op("sync", lambda y, dst=dst, bs=bs: y.dma_start(dst, bs),
                        waits=(m_cp,), inc="d")
            cast_prev[b] = m_st
            cast_marks.append(m_st)
        m_wv_cast = cast_marks[63]
        m_wq_cast = cast_marks[67]
        m_wk_cast = cast_marks[71]

        xn_store = []
        xq_store = [None] * NT
        xtile_free = [(), ()]
        prev_sq_reader = None
        prev_qd = {"out": None, "sgn": None}
        for t in range(NT):
            b = t % 2
            xa = a_xf[:, b * D:(b + 1) * D]
            m_xl = S.op("sync", lambda y, t=t, xa=xa: y.dma_start(
                xa, x_in[t * 128:(t + 1) * 128, :]),
                waits=xtile_free[b], inc="d")
            c0 = a_st[:, 64 + 4 * b + 0:64 + 4 * b + 1]
            c1 = a_st[:, 64 + 4 * b + 1:64 + 4 * b + 2]
            c3 = a_st[:, 64 + 4 * b + 3:64 + 4 * b + 4]
            c2 = a_st[:, 2 * t:2 * t + 1]
            cr = a_st[:, 2 * t + 1:2 * t + 2]
            m1 = S.op("vector", lambda v, xa=xa, c0=c0: v.reduce_sum(
                c0, xa, axis=AX), waits=(m_xl,), inc="v")
            m2 = S.op("scalar", lambda a, xa=xa: a.activation(
                a_xsq[:, :], xa, AF.Square),
                waits=(m_xl, prev_sq_reader), inc="a")
            m3 = S.op("vector", lambda v, c1=c1: v.reduce_sum(
                c1, a_xsq[:, :], axis=AX), waits=(m2,), inc="v")
            m4 = S.op("vector", lambda v, c0=c0, c2=c2: v.tensor_single_scalar(
                c2, c0, 1.0 / D, OP.mult), waits=(m1,), inc="v")
            m5 = S.op("vector", lambda v, c2=c2, c3=c3: v.tensor_mul(
                c3, c2, c2), waits=(m4,), inc="v")
            m6a = S.op("vector", lambda v, c1=c1: v.tensor_single_scalar(
                c1, c1, 1.0 / D, OP.mult), waits=(m3,), inc="v")
            m6 = S.op("vector", lambda v, c0=c0, c1=c1, c3=c3: v.tensor_tensor(
                c0, c1, c3, op=OP.subtract), waits=(m6a, m5), inc="v")
            m7 = S.op("vector", lambda v, c0=c0: v.tensor_single_scalar(
                c0, c0, 1e-5, OP.add), waits=(m6,), inc="v")
            m8 = S.op("scalar", lambda a, c0=c0, c1=c1: a.activation(
                c1, c0, AF.Ln), waits=(m7,), inc="a")
            m9 = S.op("vector", lambda v, c1=c1, c3=c3: v.tensor_single_scalar(
                c3, c1, -0.5, OP.mult), waits=(m8,), inc="v")
            m10 = S.op("scalar", lambda a, c3=c3, cr=cr: a.activation(
                cr, c3, AF.Exp), waits=(m9,), inc="a")
            xnb = a_xnbf[:, b * D:(b + 1) * D]
            m11 = S.op("vector", lambda v, xa=xa, xnb=xnb, c2=c2, cr=cr:
                       v.tensor_scalar(xnb, xa, c2, cr,
                                       op0=OP.subtract, op1=OP.mult),
                       waits=(m10, xn_store[t - 2] if t >= 2 else None),
                       inc="v")
            m12 = S.op("vector", lambda v, xa=xa, c2=c2, cr=cr:
                       v.tensor_scalar(a_xnf[:, :], xa, c2, cr,
                                       op0=OP.subtract, op1=OP.mult),
                       waits=(prev_qd["out"], prev_qd["sgn"]), inc="v")
            m_xn_st = S.op("sync", lambda y, t=t, xnb=xnb: y.dma_start(
                xn_scr[t * 128:(t + 1) * 128, :], xnb), waits=(m11,), inc="d")
            xn_store.append(m_xn_st)
            xqb = a_xqbf[:, b * D:(b + 1) * D]
            qd = emit_quant(a_xnf[:, :], xqb, D, w_in=(m12,),
                            w_out=(xq_store[t - 2] if t >= 2 else None,))
            prev_qd = qd
            prev_sq_reader = m3
            xq_store[t] = S.op("sync", lambda y, t=t, xqb=xqb: y.dma_start(
                xq_scr[t * 128:(t + 1) * 128, :], xqb),
                waits=(qd["out"],), inc="d")
            xtile_free[b] = (m11, m12, m10)

        m_xnt = None
        for c in range(DC):
            m_xnt = S.op("sync", lambda y, c=c: y.dma_start(
                a_xnt[:, c * N:(c + 1) * N],
                xn_scr[:, c * 128:(c + 1) * 128], transpose=True),
                waits=(xn_store[-1],), inc="d")

        m_wqt = m_wkt = None
        for c in range(DC):
            m_wqt = S.op("sync", lambda y, c=c: y.dma_start(
                a_wqt[:, c * 512:(c + 1) * 512],
                wq_bf[:, c * 128:(c + 1) * 128], transpose=True),
                waits=(m_wq_cast,), inc="d")
            m_wkt = S.op("sync", lambda y, c=c: y.dma_start(
                a_wkt[:, c * 512:(c + 1) * 512],
                wk_bf[:, c * 128:(c + 1) * 128], transpose=True),
                waits=(m_wk_cast,), inc="d")

        qk_store = {}
        maxabs_marks = []
        qk_pp = 0
        qk_evac_free = [None, None]
        qk_eb_free = [None, None]
        for which, wslab, scr, mtw in (("q", a_wqt, qt_scr, m_wqt),
                                       ("k", a_wkt, kt_scr, m_wkt)):
            for pc in range(4):
                for g in range(G):
                    bank = pO[qk_pp % 2]
                    eb = a_qke[:, (qk_pp % 2) * 512:(qk_pp % 2 + 1) * 512]
                    mm = None
                    for c in range(DC):
                        w0 = (mtw, m_xnt, qk_evac_free[qk_pp % 2]) if c == 0 else ()
                        mm = S.op("tensor", lambda T, bank=bank, wslab=wslab,
                                  c=c, pc=pc, g=g: T.matmul(
                                      bank[:, :],
                                      wslab[:, c * 512 + pc * 128:
                                            c * 512 + (pc + 1) * 128],
                                      a_xnt[:, c * N + g * GW:
                                            c * N + (g + 1) * GW],
                                      start=(c == 0), stop=(c == DC - 1)),
                                  waits=w0, inc="t" if c == DC - 1 else None)
                    m_ev = S.op("vector", lambda v, bank=bank, eb=eb:
                                v.tensor_copy(eb, bank[:, :]),
                                waits=(mm, qk_eb_free[qk_pp % 2]), inc="v")
                    m_st = S.op("sync", lambda y, scr=scr, pc=pc, g=g, eb=eb:
                                y.dma_start(scr[pc * 128:(pc + 1) * 128,
                                                g * GW:(g + 1) * GW], eb),
                                waits=(m_ev,), inc="d")
                    qk_store[(which, pc, g)] = m_st
                    qk_eb_free[qk_pp % 2] = m_st
                    col = pc + (0 if which == "q" else 4)
                    if g == 0:
                        mrm = S.op("vector", lambda v, eb=eb, col=col:
                                   v.tensor_reduce(
                                       smm[:, col:col + 1], eb, axis=AX,
                                       op=OP.max, apply_absolute_value=True),
                                   inc="v")
                    else:
                        m_r2 = S.op("vector", lambda v, eb=eb: v.tensor_reduce(
                            smm[:, 15:16], eb, axis=AX, op=OP.max,
                            apply_absolute_value=True), inc="v")
                        mrm = S.op("vector", lambda v, col=col: v.tensor_max(
                            smm[:, col:col + 1], smm[:, col:col + 1],
                            smm[:, 15:16]), waits=(m_r2,), inc="v")
                    qk_evac_free[qk_pp % 2] = mrm
                    if g == G - 1:
                        maxabs_marks.append(mrm)
                    qk_pp += 1
        m_qk_evac_last = qk_evac_free[(qk_pp - 1) % 2]

        m_sq = S.op("vector", lambda v: v.tensor_mul(
            smm[:, 16:24], smm[:, 0:8], smm[:, 0:8]),
            waits=tuple(maxabs_marks), inc="v")
        m_sqb = S.op("vector", lambda v: v.tensor_copy(
            cbf[:, 0:8], smm[:, 16:24]), waits=(m_sq,), inc="v")
        m_csum = None
        for idx in range(16):
            col = idx % 8
            lo = 0 if idx < 8 else 64
            m_csum = S.op("tensor", lambda T, idx=idx, col=col, lo=lo: T.matmul(
                pDen[0:1, idx:idx + 1], cbf[lo:lo + 64, col:col + 1],
                ones[lo:lo + 64, 0:1], start=(idx == 0), stop=(idx == 15),
                tile_position=(lo, 0), skip_group_check=True),
                waits=(m_sqb, m_ones), inc="t")
        m_cs = S.op("vector", lambda v: v.tensor_copy(
            smm[0:1, 24:40], pDen[0:1, 0:16]), waits=(m_csum,), inc="v")
        m_ch = []
        for h in range(H):
            pc2, par = h // 2, h % 2
            qc = 24 + par * 8 + pc2
            kc = 24 + par * 8 + 4 + pc2
            ma1 = S.op("vector", lambda v, h=h, qc=qc, kc=kc: v.tensor_add(
                smm[0:1, 40 + h:41 + h], smm[0:1, qc:qc + 1],
                smm[0:1, kc:kc + 1]), waits=(m_cs,), inc="v")
            ma2 = S.op("vector", lambda v, h=h: v.tensor_single_scalar(
                smm[0:1, 40 + h:41 + h], smm[0:1, 40 + h:41 + h], -0.5,
                OP.mult), waits=(ma1,), inc="v")
            m_ch.append(S.op("vector", lambda v, h=h: v.tensor_copy(
                cbf[0:1, 8 + h:9 + h], smm[0:1, 40 + h:41 + h]),
                waits=(ma2,), inc="v"))
        m_cneg = []
        prev_cp = m_cs
        for h in range(H):
            mM = S.op("tensor", lambda T, h=h: T.matmul(
                pDen[:, 0:1], ones[0:1, :], cbf[0:1, 8 + h:9 + h],
                start=True, stop=True, skip_group_check=True),
                waits=(m_ch[h], prev_cp), inc="t")
            mC = S.op("scalar", lambda a, h=h: a.copy(
                cneg[:, h:h + 1], pDen[:, 0:1]), waits=(mM,), inc="a")
            prev_cp = mC
            m_cneg.append(mC)

        v_spilled = {}
        wvt_free = [None, None]
        vout_store = None
        v_evac_last = None
        v_evac_waits = {}
        for h in range(H):
            wb = h % 2
            m_wvt = None
            for c in range(DC):
                m_wvt = S.op("sync", lambda y, h=h, c=c, wb=wb: y.dma_start(
                    a_wvt[:, wb * DC * D + c * D:wb * DC * D + (c + 1) * D],
                    wv_bf[h * D:(h + 1) * D, c * 128:(c + 1) * 128],
                    transpose=True),
                    waits=(m_wv_cast, wvt_free[wb]), inc="d")
            mm = None
            for mc in range(NT):
                k8 = mc % 8
                pa = pS[2 * (mc % 2)]
                pb2 = pS[2 * (mc % 2) + 1]
                for c in range(DC):
                    w0 = ()
                    if c == 0:
                        w0 = (m_wvt, m_xnt, v_evac_waits.get(mc % 2))
                    S.op("tensor", lambda T, pa=pa, c=c, mc=mc, wb=wb: T.matmul(
                        pa[:, :],
                        a_xnt[:, c * N + mc * 128:c * N + (mc + 1) * 128],
                        a_wvt[:, wb * DC * D + c * D:wb * DC * D + c * D + 512],
                        start=(c == 0), stop=(c == DC - 1)), waits=w0)
                    mm = S.op("tensor", lambda T, pb2=pb2, c=c, mc=mc, wb=wb:
                              T.matmul(
                                  pb2[:, :],
                                  a_xnt[:, c * N + mc * 128:c * N + (mc + 1) * 128],
                                  a_wvt[:, wb * DC * D + c * D + 512:
                                        wb * DC * D + (c + 1) * D],
                                  start=(c == 0), stop=(c == DC - 1)),
                              inc="t" if c == DC - 1 else None, no_ldw=True)
                d0 = a_vout[:, 0 * 4096 + k8 * 512:0 * 4096 + (k8 + 1) * 512]
                d1 = a_vout[:, 1 * 4096 + k8 * 512:1 * 4096 + (k8 + 1) * 512]
                S.op("vector", lambda v, pa=pa, d0=d0: v.tensor_copy(
                    d0, pa[:, :]),
                    waits=(mm, vout_store if k8 == 0 else None), inc="v")
                m_ev = S.op("vector", lambda v, pb2=pb2, d1=d1: v.tensor_copy(
                    d1, pb2[:, :]), inc="v")
                v_evac_waits[mc % 2] = m_ev
                v_evac_last = m_ev
                if k8 == 7:
                    blk8 = mc // 8
                    for half in range(2):
                        vout_store = S.op(
                            "sync", lambda y, h=h, half=half, blk8=blk8:
                            y.dma_start(
                                v_scr[h, half, :, blk8 * 8:(blk8 + 1) * 8, :],
                                a_vout[:, half * 4096:(half + 1) * 4096]
                                .rearrange("p (k d) -> p k d", d=512)),
                            waits=(m_ev,), inc="d")
                    v_spilled[(h, blk8)] = vout_store
            wvt_free[wb] = mm

        if debug:
            dbg_c = nc.dram_tensor("dbg_c", [128, 64 + 8], F32,
                                   kind="ExternalOutput").ap()
            S.op("sync", lambda y: y.dma_start(dbg_c[:, 0:8], cneg[:, :]),
                 waits=(m_cneg[-1],), inc="d")
            S.op("sync", lambda y: y.dma_start(dbg_c[:, 8:72], smm[:, :]),
                 inc="d")
            dbg_a = nc.dram_tensor("dbg_a", [128, 80], F32,
                                   kind="ExternalOutput").ap()
            dbg_xnf = nc.dram_tensor("dbg_xnf", [128, D], F32,
                                     kind="ExternalOutput").ap()
            dbg_xqb = nc.dram_tensor("dbg_xqb", [128, 2 * D], BF16,
                                     kind="ExternalOutput").ap()
            S.op("sync", lambda y: y.dma_start(dbg_a, a_st[:, :]),
                 waits=(prev_qd["out"],), inc="d")
            S.op("sync", lambda y: y.dma_start(dbg_xnf, a_xnf[:, :]), inc="d")
            S.op("sync", lambda y: y.dma_start(dbg_xqb, a_xqbf[:, :]), inc="d")
            dbg_qp = nc.dram_tensor("dbg_qp", [128, 5 * 1024 + 3 * 176], F32,
                                    kind="ExternalOutput").ap()
            for qi in range(3):
                S.op("sync", lambda y, qi=qi: y.dma_start(
                    dbg_qp[:, qi * 1024:(qi + 1) * 1024], qp[qi][:, :]), inc="d")
            S.op("sync", lambda y: y.dma_start(
                dbg_qp[:, 5120:5296], thrp[:, :]), inc="d")
            S.op("sync", lambda y: y.dma_start(
                dbg_qp[:, 5296:5472], thr6[:, :]), inc="d")
            S.op("sync", lambda y: y.dma_start(
                dbg_qp[:, 5472:5648], tks[0][:, :]), inc="d")

        S.fence()
        stA.close()
        stB = ExitStack()
        e = stB.enter_context
        b_kt = e(nc.sbuf_tensor("b_kt", [128, 2 * N], BF16))
        b_qt = e(nc.sbuf_tensor("b_qt", [128, 2 * 512], BF16))
        b_et = e(nc.sbuf_tensor("b_et", [128, 2 * NT * 512], BF16))
        b_vb = e(nc.sbuf_tensor("b_vb", [128, 2 * NT * 512], BF16))
        b_oacc = e(nc.sbuf_tensor("b_oacc", [128, 4 * D], F32))
        b_rden = e(nc.sbuf_tensor("b_rden", [128, 8], F32))
        b_wpf = e(nc.sbuf_tensor("b_wpf", [128, D], F32))
        b_wpb = e(nc.sbuf_tensor("b_wpb", [128, D], BF16))

        wprep_jobs = []
        for s in range(22):
            wprep_jobs.append((wg_in[s * 128:(s + 1) * 128, :],
                               wgq_scr[s * 128:(s + 1) * 128, :]))
        for s in range(22):
            wprep_jobs.append((wu_in[s * 128:(s + 1) * 128, :],
                               wuq_scr[s * 128:(s + 1) * 128, :]))
        for s in range(8):
            for sec in range(3):
                w = 1024 if sec < 2 else 768
                wprep_jobs.append((wd_in[s * 128:(s + 1) * 128,
                                         sec * 1024:sec * 1024 + w],
                                   wdq_scr[s * 128:(s + 1) * 128,
                                           sec * 1024:sec * 1024 + w]))
        wprep_i = [0]
        wprep_st = [None]
        wprep_qd = [{"out": None, "sgn": None}]

        def emit_wprep_one():
            if wprep_i[0] >= len(wprep_jobs):
                return
            src, dst = wprep_jobs[wprep_i[0]]
            W = src.shape[1]
            wprep_i[0] += 1
            m_ld = S.op("sync", lambda y, src=src, W=W: y.dma_start(
                b_wpf[:, 0:W], src),
                waits=(wprep_qd[0]["out"], wprep_qd[0]["sgn"]), inc="d")
            qd = emit_quant(b_wpf[:, 0:W], b_wpb[:, 0:W], W,
                            w_in=(m_ld,), w_out=(wprep_st[0],))
            wprep_qd[0] = qd
            wprep_st[0] = S.op("sync", lambda y, dst=dst, W=W: y.dma_start(
                dst, b_wpb[:, 0:W]), waits=(qd["out"],), inc="d")

        kt_free = [m_qk_evac_last, m_qk_evac_last]
        vb_free = [v_evac_last, v_evac_last]
        pO_free = {0: m_qk_evac_last, 1: m_qk_evac_last, 2: None}
        pden_free = m_cneg[-1]
        et_consumed = [v_evac_last, v_evac_last]
        oacc_stored = None
        o_rc = [0]
        exp_marks = None

        for g in range(G):
            for p in range(4):
                pb = p % 2
                m_kt = S.op("sync", lambda y, p=p, pb=pb: y.dma_start(
                    b_kt[:, pb * N:(pb + 1) * N],
                    kt_scr[p * 128:(p + 1) * 128, :]),
                    waits=(qk_store[("k", p, G - 1)], kt_free[pb]), inc="d")
                m_qt = S.op("sync", lambda y, p=p, g=g, pb=pb: y.dma_start(
                    b_qt[:, pb * 512:(pb + 1) * 512],
                    qt_scr[p * 128:(p + 1) * 128, g * GW:(g + 1) * GW]),
                    waits=(qk_store[("q", p, g)],), inc="d")
                exp_marks = {0: {}, 1: {}}
                m_s = None
                for mc in range(NT):
                    pA = pS[mc % 2]
                    pB = pS[2 + mc % 2]
                    wA = [m_kt, m_qt]
                    if mc >= 2:
                        wA += [exp_marks[0][mc - 2], exp_marks[1][mc - 2]]
                    else:
                        wA += [et_consumed[0], et_consumed[1]]
                    S.op("tensor", lambda T, pA=pA, pb=pb, mc=mc: T.matmul(
                        pA[:, :],
                        b_kt[0:64, pb * N + mc * 128:pb * N + (mc + 1) * 128],
                        b_qt[0:64, pb * 512:(pb + 1) * 512],
                        start=True, stop=True), waits=tuple(wA))
                    m_s = S.op("tensor", lambda T, pB=pB, pb=pb, mc=mc: T.matmul(
                        pB[:, :],
                        b_kt[64:128, pb * N + mc * 128:pb * N + (mc + 1) * 128],
                        b_qt[64:128, pb * 512:(pb + 1) * 512],
                        start=True, stop=True, tile_position=(64, 0)), inc="t")
                    for hh in range(2):
                        h = 2 * p + hh
                        psrc = pA if hh == 0 else pB
                        exp_marks[hh][mc] = S.op(
                            "scalar", lambda a, hh=hh, mc=mc, psrc=psrc, h=h:
                            a.activation(
                                b_et[:, hh * NT * 512 + mc * 512:
                                     hh * NT * 512 + (mc + 1) * 512],
                                psrc[:, :], AF.Exp, bias=cneg[:, h:h + 1]),
                            waits=(m_s, m_cneg[h]), inc="a")
                kt_free[pb] = m_s
                for hh in range(2):
                    h = 2 * p + hh
                    m_exp_last = exp_marks[hh][NT - 1]
                    mm_last = None
                    m_vl = {}
                    m_rc_sp = {}
                    for sp in range(2):
                        for half in range(2):
                            vbuf = half
                            if sp == 0:
                                m_vl[half] = S.op(
                                    "sync", lambda y, h=h, half=half, vbuf=vbuf:
                                    y.dma_start(
                                        b_vb[:, vbuf * NT * 512:
                                             (vbuf + 1) * NT * 512],
                                        v_scr[h, half, :, :, :].rearrange(
                                            "p k d -> p (k d)")),
                                    waits=(v_spilled[(h, 3)], vb_free[vbuf]),
                                    inc="d")
                            rc = o_rc[0]
                            o_rc[0] += 1
                            banks = (pO[(2 * rc) % 3], pO[(2 * rc + 1) % 3])
                            for mc in range(NT):
                                for si, sub in enumerate((2 * sp, 2 * sp + 1)):
                                    bk = banks[si]
                                    w0 = ()
                                    if mc == 0:
                                        w0 = (m_exp_last, m_vl[half],
                                              pO_free[(2 * rc + si) % 3])
                                        if half == 0 and si == 0:
                                            w0 = w0 + ((pden_free,) if sp == 0
                                                       else (m_rc_sp[0],))
                                    is_last = (mc == NT - 1 and si == 1)
                                    mm_last = S.op(
                                        "tensor",
                                        lambda T, bk=bk, hh=hh, mc=mc, sub=sub,
                                        vbuf=vbuf: T.matmul(
                                            bk[:, :],
                                            b_et[:, hh * NT * 512 + mc * 512
                                                 + sub * 128:
                                                 hh * NT * 512 + mc * 512
                                                 + (sub + 1) * 128],
                                            b_vb[:, vbuf * NT * 512 + mc * 512:
                                                 vbuf * NT * 512
                                                 + (mc + 1) * 512],
                                            start=(mc == 0),
                                            stop=(mc == NT - 1)),
                                        waits=w0,
                                        inc="t" if (is_last and half == 1)
                                        else None)
                                    if half == 0:
                                        m_rid = S.op(
                                            "tensor", lambda T, hh=hh, mc=mc,
                                            sub=sub, sp=sp: T.matmul(
                                                pDen[:, sub:sub + 1],
                                                b_et[:, hh * NT * 512
                                                     + mc * 512 + sub * 128:
                                                     hh * NT * 512 + mc * 512
                                                     + (sub + 1) * 128],
                                                ones[:, 0:1],
                                                start=(mc == 0 and sub == 0
                                                       and sp == 0),
                                                stop=(mc == NT - 1 and sub == 3),
                                                skip_group_check=True),
                                            inc="t" if is_last else None,
                                            no_ldw=True)
                                        if is_last:
                                            mm_last = m_rid
                            if half == 0:
                                m_rc_sp[sp] = S.op(
                                    "vector", lambda v, sp=sp: v.reciprocal(
                                        b_rden[:, 2 * sp:2 * sp + 2],
                                        pDen[:, 2 * sp:2 * sp + 2]),
                                    waits=(mm_last,), inc="v")
                                if sp == 1:
                                    pden_free = m_rc_sp[1]
                            for si, sub in enumerate((2 * sp, 2 * sp + 1)):
                                bk = banks[si]
                                dst = b_oacc[:, sub * D + half * 512:
                                             sub * D + half * 512 + 512]
                                w0 = (mm_last, m_rc_sp[sp], oacc_stored)
                                if h == 0:
                                    m_ev = S.op(
                                        "vector", lambda v, bk=bk, dst=dst,
                                        sub=sub: v.tensor_scalar(
                                            dst, bk[:, :],
                                            b_rden[:, sub:sub + 1], None,
                                            op0=OP.mult),
                                        waits=w0, inc="v")
                                else:
                                    m_ev = S.op(
                                        "vector", lambda v, bk=bk, dst=dst,
                                        sub=sub: v.scalar_tensor_tensor(
                                            dst, bk[:, :],
                                            b_rden[:, sub:sub + 1], dst,
                                            op0=OP.mult, op1=OP.add),
                                        waits=w0, inc="v")
                                pO_free[(2 * rc + si) % 3] = m_ev
                            if sp == 1:
                                vb_free[vbuf] = mm_last
                    et_consumed[hh] = mm_last
                emit_wprep_one()
                emit_wprep_one()
            last_ev = pO_free[(2 * (o_rc[0] - 1) + 1) % 3]
            for sub in range(4):
                oacc_stored = S.op("sync", lambda y, g=g, sub=sub: y.dma_start(
                    oacc_scr[g * GW + sub * 128:g * GW + (sub + 1) * 128, :],
                    b_oacc[:, sub * D:(sub + 1) * D]),
                    waits=(last_ev,), inc="d")
        while wprep_i[0] < len(wprep_jobs):
            emit_wprep_one()
        m_wprep_done = wprep_st[0]
        b_last_exp = [exp_marks[0][NT - 1], exp_marks[1][NT - 1]]

        S.fence()
        stB.close()
        stC = ExitStack()
        e = stC.enter_context
        c_wg = e(nc.sbuf_tensor("c_wg", [128, DC * I], BF16))
        c_wu = e(nc.sbuf_tensor("c_wu", [128, DC * I], BF16))
        c_wdt = e(nc.sbuf_tensor("c_wdt", [128, IC * D], BF16))
        c_xqt = e(nc.sbuf_tensor("c_xqt", [128, 2 * DC * 128], BF16))
        c_silu = e(nc.sbuf_tensor("c_silu", [128, 512], F32))
        c_hf = e(nc.sbuf_tensor("c_hf", [128, 512], F32))
        c_hq = e(nc.sbuf_tensor("c_hq", [128, 2 * I], BF16))
        c_hqt = e(nc.sbuf_tensor("c_hqt", [128, 2 * IC * 128], BF16))
        c_xo = e(nc.sbuf_tensor("c_xo", [128, 2 * D], F32))
        c_out = e(nc.sbuf_tensor("c_out", [128, D], F32))

        m_cw = None
        for c in range(DC):
            m_cw = S.op("sync", lambda y, c=c: y.dma_start(
                c_wg[:, c * I:(c + 1) * I],
                wgq_scr[:, c * 128:(c + 1) * 128], transpose=True),
                waits=(m_wprep_done,), inc="d")
            m_cw = S.op("sync", lambda y, c=c: y.dma_start(
                c_wu[:, c * I:(c + 1) * I],
                wuq_scr[:, c * 128:(c + 1) * 128], transpose=True), inc="d")
        for ic in range(IC):
            m_cw = S.op("sync", lambda y, ic=ic: y.dma_start(
                c_wdt[:, ic * D:(ic + 1) * D],
                wdq_scr[:, ic * 128:(ic + 1) * 128], transpose=True), inc="d")

        SEGS = [(0, 512), (512, 1024), (1024, 1536), (1536, 2048),
                (2048, 2560), (2560, 2816)]
        xqt_free = [b_last_exp[0], b_last_exp[1]]
        xo_free = None
        hq_seg_store = [[None] * len([1]*6) for _ in range(2)]
        hqt_free = [None, None]
        out_stored = None
        seg_ps_free = {0: b_last_exp[0], 1: b_last_exp[0],
                       2: b_last_exp[1], 3: b_last_exp[1]}
        down_free = {0: None, 1: None}
        prev_cq = [{"out": None, "sgn": None}]
        for nb in range(NT):
            b = nb % 2
            m_xqt = None
            for c in range(DC):
                m_xqt = S.op("sync", lambda y, nb=nb, c=c, b=b: y.dma_start(
                    c_xqt[:, b * DC * 128 + c * 128:b * DC * 128 + (c + 1) * 128],
                    xq_scr[nb * 128:(nb + 1) * 128, c * 128:(c + 1) * 128],
                    transpose=True),
                    waits=(xq_store[nb], xqt_free[b]), inc="d")
            m_x = S.op("sync", lambda y, nb=nb: y.dma_start(
                c_xo[:, 0:D], x_in[nb * 128:(nb + 1) * 128, :]),
                waits=(xo_free,), inc="d")
            m_o = S.op("sync", lambda y, nb=nb: y.dma_start(
                c_xo[:, D:2 * D], oacc_scr[nb * 128:(nb + 1) * 128, :]),
                waits=(oacc_stored,), inc="d")
            m_seg_out = []
            mm_u = None
            m_hqt_ic = [None] * IC
            for si, (s0, s1) in enumerate(SEGS):
                W = s1 - s0
                pg = pS[2 * (si % 2)]
                pu = pS[2 * (si % 2) + 1]
                for c in range(DC):
                    w0 = ()
                    if c == 0:
                        w0 = (m_xqt, m_cw, seg_ps_free[2 * (si % 2)],
                              seg_ps_free[2 * (si % 2) + 1])
                    S.op("tensor", lambda T, pg=pg, c=c, s0=s0, s1=s1, b=b:
                         T.matmul(pg[:, 0:s1 - s0],
                                  c_xqt[:, b * DC * 128 + c * 128:
                                        b * DC * 128 + (c + 1) * 128],
                                  c_wg[:, c * I + s0:c * I + s1],
                                  start=(c == 0), stop=(c == DC - 1)),
                         waits=w0)
                    mm_u = S.op("tensor", lambda T, pu=pu, c=c, s0=s0, s1=s1,
                                b=b: T.matmul(
                                    pu[:, 0:s1 - s0],
                                    c_xqt[:, b * DC * 128 + c * 128:
                                          b * DC * 128 + (c + 1) * 128],
                                    c_wu[:, c * I + s0:c * I + s1],
                                    start=(c == 0), stop=(c == DC - 1)),
                                inc="t" if c == DC - 1 else None, no_ldw=True)
                m_si = S.op("scalar", lambda a, pg=pg, W=W: a.activation(
                    c_silu[:, 0:W], pg[:, 0:W], AF.Silu),
                    waits=(mm_u, prev_cq[0]["out"]), inc="a")
                m_h = S.op("vector", lambda v, pu=pu, W=W: v.tensor_mul(
                    c_hf[:, 0:W], c_silu[:, 0:W], pu[:, 0:W]),
                    waits=(m_si, prev_cq[0]["out"]), inc="v")
                seg_ps_free[2 * (si % 2)] = m_h
                seg_ps_free[2 * (si % 2) + 1] = m_h
                qd = emit_quant(c_hf[:, 0:W], c_hq[:, b * I + s0:b * I + s1],
                                W, w_in=(m_h,),
                                w_out=(hq_seg_store[b][si],))
                prev_cq[0] = qd
                m_seg_out.append(qd["out"])
                m_sst = S.op("sync", lambda y, nb=nb, s0=s0, s1=s1, b=b:
                             y.dma_start(
                                 hq_scr[nb * 128:(nb + 1) * 128, s0:s1],
                                 c_hq[:, b * I + s0:b * I + s1]),
                             waits=(qd["out"],), inc="d")
                hq_seg_store[b][si] = m_sst
                ic0, ic1 = s0 // 128, s1 // 128
                for ic in range(ic0, ic1):
                    m_hqt_ic[ic] = S.op(
                        "sync", lambda y, nb=nb, ic=ic, b=b: y.dma_start(
                            c_hqt[:, b * IC * 128 + ic * 128:
                                  b * IC * 128 + (ic + 1) * 128],
                            hq_scr[nb * 128:(nb + 1) * 128,
                                   ic * 128:(ic + 1) * 128],
                            transpose=True),
                        waits=(m_sst, hqt_free[b]), inc="d")
            mm_d = None
            for ic in range(IC):
                w0 = (m_hqt_ic[ic],)
                if ic == 0:
                    w0 = w0 + (down_free[0], down_free[1])
                S.op("tensor", lambda T, ic=ic, b=b: T.matmul(
                    pO[0][:, :],
                    c_hqt[:, b * IC * 128 + ic * 128:b * IC * 128 + (ic + 1) * 128],
                    c_wdt[:, ic * D:ic * D + 512],
                    start=(ic == 0), stop=(ic == IC - 1)), waits=w0)
                mm_d = S.op("tensor", lambda T, ic=ic, b=b: T.matmul(
                    pO[1][:, :],
                    c_hqt[:, b * IC * 128 + ic * 128:b * IC * 128 + (ic + 1) * 128],
                    c_wdt[:, ic * D + 512:(ic + 1) * D],
                    start=(ic == 0), stop=(ic == IC - 1)),
                    inc="t" if ic == IC - 1 else None, no_ldw=True)
            hqt_free[b] = mm_d
            m_xo2 = S.op("vector", lambda v: v.tensor_add(
                c_xo[:, 0:D], c_xo[:, 0:D], c_xo[:, D:2 * D]),
                waits=(m_x, m_o), inc="v")
            mo0 = S.op("vector", lambda v: v.tensor_add(
                c_out[:, 0:512], pO[0][:, :], c_xo[:, 0:512]),
                waits=(mm_d, m_xo2, out_stored), inc="v")
            mo1 = S.op("vector", lambda v: v.tensor_add(
                c_out[:, 512:D], pO[1][:, :], c_xo[:, 512:D]), inc="v")
            down_free[0] = mo0
            down_free[1] = mo1
            xo_free = mo1
            xqt_free[b] = mm_u
            out_stored = S.op("sync", lambda y, nb=nb: y.dma_start(
                out_t[nb * 128:(nb + 1) * 128, :], c_out[:, :]),
                waits=(mo1,), inc="d")

        S.op("sync", lambda y: None, waits=(("d", S.cnt["d"]),))
        stC.close()

        _EMIT_GUARD = True
        with (nc.semaphore("sem_d") as hd,
              nc.semaphore("sem_t") as ht,
              nc.semaphore("sem_v") as hv,
              nc.semaphore("sem_a") as ha,
              nc.Block() as blk):
            handles = {"d": hd, "t": ht, "v": hv, "a": ha}

            @blk.sync
            def _(eng):
                S.emit("sync", eng, handles)

            @blk.tensor
            def _(eng):
                S.emit("tensor", eng, handles)

            @blk.vector
            def _(eng):
                S.emit("vector", eng, handles)

            @blk.scalar
            def _(eng):
                S.emit("scalar", eng, handles)

    import dataclasses
    names = getattr(S, "no_ldw_names", set())
    if names:
        for f in nc.m.functions:
            for bb in f.blocks:
                insts = bb.instructions
                for i, ins in enumerate(insts):
                    if isinstance(ins, mybir.InstMatmult) and ins.name in names:
                        insts[i] = dataclasses.replace(ins, ldweights=False)
    return nc


_CACHE = {}


def kernel(L0_attn_x, L0_attn_Wq, L0_attn_Wk, L0_attn_Wv,
           L0_ffn_w_gate, L0_ffn_w_up, L0_ffn_w_down):
    if "nc" not in _CACHE:
        _CACHE["nc"] = build_program()
    nc = _CACHE["nc"]
    x = np.ascontiguousarray(np.asarray(L0_attn_x, dtype=np.float32))
    in_common = dict(
        wq=np.ascontiguousarray(np.asarray(L0_attn_Wq, dtype=np.float32)),
        wk=np.ascontiguousarray(np.asarray(L0_attn_Wk, dtype=np.float32)),
        wv=np.ascontiguousarray(np.asarray(L0_attn_Wv, dtype=np.float32)),
        wg=np.ascontiguousarray(np.asarray(L0_ffn_w_gate, dtype=np.float32)),
        wu=np.ascontiguousarray(np.asarray(L0_ffn_w_up, dtype=np.float32)),
        wd=np.ascontiguousarray(np.asarray(L0_ffn_w_down, dtype=np.float32)))
    in_maps = [dict(x=np.ascontiguousarray(x[i]), **in_common)
               for i in range(8)]
    res = run_bass_kernel_spmd(nc, in_maps, core_ids=list(range(8)))
    return np.stack([np.asarray(r["out"], dtype=np.float32)
                     for r in res.results], axis=0)
